# version 10
# baseline (speedup 1.0000x reference)
import contextlib
import os
import numpy as np
import ml_dtypes
import concourse.bass as bass
import concourse.mybir as mybir
from concourse.bass_utils import run_bass_kernel_spmd

F32 = mybir.dt.float32
BF16 = mybir.dt.bfloat16
F16 = mybir.dt.float16
I32 = mybir.dt.int32
AF = mybir.ActivationFunctionType
OP = mybir.AluOpType
AX = mybir.AxisListType

B = 2
L = 8192
D = 2048
NST = 16
KC = 4
I_FF = 5632
NBLK = 8
TBLK = 256
NSTAGE = 4
TSTG = 2048
DSL = 256
EPS_LN = 1e-5
EPS_RMS = 1e-6
BIG = float(2.0 ** 126)
NCORES = 8


class Sem:
    __slots__ = ("h", "n")

    def __init__(self, h):
        self.h = h
        self.n = 0

    def bump(self, k=1):
        self.n += k
        return self.h


def build(debug_taps=(), phases=6, prelvl=5):
    nc = bass.Bass("TRN2", target_bir_lowering=False)
    es = contextlib.ExitStack()

    def din(name, shape, dt=F32):
        return nc.dram_tensor(name, shape, dt, kind="ExternalInput").ap()

    x_blk = din("x_blk", [NBLK, 384, D])
    cw_in = din("cw", [16, 128, KC])
    cb_in = din("cb", [128, 16])
    xw_in = din("xw_t", [16, 128, 160])
    dtw_in = din("dtw_t", [128, DSL])
    dtb_in = din("dtb", [128, 2])
    alog_in = din("alog", [2, 128, NST])
    dsk_in = din("dskip", [128, 2])
    gup_in = din("gup", [1408, 256], I32)
    gsc_in = din("gsc", [11, 128, 16], F16)
    dnp_in = din("dnp", [256, 704], I32)
    dsc_in = din("dsc", [2, 128, 44], F16)
    idf_in = din("id_f32", [128, 128])
    idb_in = din("id_b16", [128, 128], BF16)
    ones_in = din("ones_b16", [1, 128], BF16)
    onesc_in = din("onescol_f32", [128, 1])
    onesr_in = din("onesrow_f32", [1, 128])
    onescb_in = din("onescol_b16", [128, 1], BF16)
    eps_in = din("eps", [128, 2])

    out_ext = nc.dram_tensor("out", [NBLK, TBLK, D], F32, kind="ExternalOutput").ap()

    XDBL_b = [nc.dram_tensor(f"xdbl_cc{l}", [160, TBLK], F32) for l in range(NBLK)]
    XDBL_shb = [nc.dram_tensor(f"xdbl_sh{l}", [8 * 160, TBLK], F32,
                               addr_space="Shared") for l in range(NBLK)]
    U_b = [nc.dram_tensor(f"u_cc{l}", [D, TBLK], BF16) for l in range(NBLK)]
    U_shb = [nc.dram_tensor(f"u_sh{l}", [8 * D, TBLK], BF16, addr_space="Shared")
             for l in range(NBLK)]
    W_gu_cc = nc.dram_tensor("wgu_cc", [128, 176 * 128], BF16)
    W_gu_sh = nc.dram_tensor("wgu_sh", [8 * 128, 176 * 128], BF16, addr_space="Shared")
    W_dn_cc = nc.dram_tensor("wdn_cc", [128, 88 * 128], BF16)
    W_dn_sh = nc.dram_tensor("wdn_sh", [8 * 128, 88 * 128], BF16, addr_space="Shared")
    XSP = nc.dram_tensor("xspill", [NBLK, D, TBLK], F32)
    BC_stage = nc.dram_tensor("bc_stage", [2, B, 32, TSTG], F32)
    Z_ccu = [[nc.dram_tensor(f"z_cc{k}_{u}", [128, TSTG], F32) for u in range(4)]
             for k in range(NSTAGE)]
    Z_shu = [[nc.dram_tensor(f"z_sh{k}_{u}", [8 * 128, TSTG], F32,
                             addr_space="Shared") for u in range(4)]
             for k in range(NSTAGE)]

    dbg = {}
    for name, shape, dt in debug_taps:
        dbg[name] = nc.dram_tensor("dbg_" + name, list(shape), dt,
                                   kind="ExternalOutput").ap()

    with es:
        def sbuf(name, shape, dt):
            return es.enter_context(nc.sbuf_tensor(name, shape, dt))

        c_cw = sbuf("c_cw", [128, 16 * KC], F32)
        c_cb = sbuf("c_cb", [128, 16], F32)
        c_xwb = sbuf("c_xwb", [128, 16 * 160], BF16)
        c_dtwb = sbuf("c_dtwb", [128, DSL], BF16)
        c_dtb = sbuf("c_dtb", [128, 2], F32)
        c_A = sbuf("c_A", [128, 2 * NST], F32)
        c_dsk = sbuf("c_dsk", [128, 2], F32)
        c_idf = sbuf("c_idf", [128, 128], F32)
        c_idb = sbuf("c_idb", [128, 128], BF16)
        c_ones = sbuf("c_ones", [1, 128], BF16)
        c_onesc = sbuf("c_onesc", [128, 1], F32)
        c_onesr = sbuf("c_onesr", [1, 128], F32)
        c_onescb = sbuf("c_onescb", [128, 1], BF16)
        c_tmp = sbuf("c_tmp", [128, 2 * 160], F32)
        state = sbuf("state", [128, B * 2 * NST], F32)
        q_gsc = sbuf("q_gsc", [128, 11 * 16], F32)
        q_dsc = sbuf("q_dsc", [128, 2 * 44], F32)
        q_scs = sbuf("q_scs", [128, 11 * 16], F16)
        q_dss = sbuf("q_dss", [128, 2 * 44], F16)
        f_cols = sbuf("f_cols", [128, 64], F32)
        c_eps = sbuf("c_eps", [128, 2], F32)

        arena = sbuf("arena", [128, 47360], F32)
        A32 = arena.ap()
        A16 = arena.ap().bitcast(BF16)
        AI = arena.ap().bitcast(I32)

        def cv32(off, n):
            assert off % 4 == 0 and off + 4 * n <= 189440
            return A32[:, off // 4: off // 4 + n]

        def cv16(off, n):
            assert off % 4 == 0 and off + 2 * n <= 189440
            return A16[:, off // 2: off // 2 + n]

        def cvi(off, n):
            assert off % 4 == 0 and off + 4 * n <= 189440
            return AI[:, off // 4: off // 4 + n]

        p_xtm = cv32(0, 6144)
        p_xfm = cv32(24576, 6144)
        p_ub = cv16(49152, 4096)
        p_xdbl = cv32(57344, 2048)
        p_xbc = cv32(65536, 2048)
        p_csc = cv32(73728, 256)
        q_pk = cvi(0, 704)
        q_t0 = cvi(4096, 5632)
        q_bits = cvi(26624, 5632)
        q_deq = cv32(49152, 5632)
        q_wt = cv16(71680, 5632)
        m_dt = cv32(0, 4096)
        m_dtb = cv16(16384, 4096)
        m_dBu = cv32(24576, 4096)
        m_Cb2 = (cv32(40960, 2048), cv32(131072, 2048))
        m_delta = cv16(49152, 8192)
        m_u = cv16(65536, 8192)
        m_du = cv16(81920, 8192)
        m_sp = cv32(98304, 2048)
        m_rel = cv32(106496, 2048)
        m_dBu2 = (m_sp, m_rel)
        m_dA = cv32(114688, 4096)
        m_h2 = (cv32(139264, 2048), cv32(155648, 2048))
        m_hC = cv16(147456, 4096)
        m_z = cv32(163840, 2048)
        f_bigA = cv32(0, 8192)
        f_x1tm = cv16(32768, 8192)
        f_x1b = cv16(49152, 8192)
        f_hh = cv16(65536, 22528)
        f_wgu = cv16(110592, 8192)
        f_wdn = cv16(126976, 11264)
        f_y2b = cv16(110592, 8192)
        f_r1sq = cv16(149504, 8192)
        f_out2 = cv32(165888, 4096)
        f_sg = cv16(182272, 512)
        f_x1t = cv32(183296, 512)
        f_sqd = cv16(185344, 2048)
        f_rows = A32[0:1, 16384:16384 + 2560]

        pMix = es.enter_context(nc.psum_tensor("pMix", [128, 2048], F32))
        pBh = es.enter_context(nc.psum_tensor("pBh", [128, 1024], F32))
        pCh = es.enter_context(nc.psum_tensor("pCh", [128, 1024], F32))

        S = {}
        for s in ["cld", "cprep", "xld", "xtr", "xev", "cv", "xpe", "xp", "ud",
                  "xdd", "xsd", "qld", "qsh", "qdq", "qtr", "qev", "qwd", "ag",
                  "mld", "mdtb", "mdp", "msp", "mdel", "mdu", "mbb", "mcb",
                  "mda", "mbu", "msc", "mhc", "myq", "mze", "mzd",
                  "fzl", "fxl", "fr1", "fmq", "fmo", "frw", "fbc", "fx1", "fxb",
                  "ftr", "fte", "fwg", "fgu", "fsl", "fhh", "fwd", "fdn", "fde",
                  "fc2", "fr2", "fo2", "fod", "dbgs", "msv", "mrw",
                  "mbb1", "mcb1", "mhcg"]:
            S[s] = Sem(es.enter_context(nc.semaphore("s_" + s)))
        (cld, cprep, xld, xtr, xev, cv, xpe, xp, ud, xdd, xsd, qld, qsh, qdq,
         qtr, qev, qwd, ag, mld, mdtb, mdp, msp, mdel, mdu, mbb, mcb, mda, mbu,
         msc, mhc, myq, mze, mzd, fzl, fxl, fr1, fmq, fmo, frw, fbc, fx1, fxb,
         ftr, fte, fwg, fgu, fsl, fhh, fwd, fdn, fde, fc2, fr2, fo2, fod) = (
            S[s] for s in ["cld", "cprep", "xld", "xtr", "xev", "cv", "xpe",
                           "xp", "ud", "xdd", "xsd", "qld", "qsh", "qdq", "qtr",
                           "qev", "qwd", "ag", "mld", "mdtb", "mdp", "msp",
                           "mdel", "mdu", "mbb", "mcb", "mda", "mbu", "msc",
                           "mhc", "myq", "mze", "mzd", "fzl", "fxl", "fr1",
                           "fmq", "fmo", "frw", "fbc", "fx1", "fxb", "ftr",
                           "fte", "fwg", "fgu", "fsl", "fhh", "fwd", "fdn",
                           "fde", "fc2", "fr2", "fo2", "fod"])

        dbgs = S["dbgs"]
        msv = S["msv"]
        mrw = S["mrw"]
        mbbP = (S["mbb"], S["mbb1"])
        mcbP = (S["mcb"], S["mcb1"])
        mhcP = (S["mhc"], S["mhcg"])
        TRIG_DONE = {}
        blk_ctx = es.enter_context(nc.Block())

        class EngTrack:

            def __init__(self, e):
                object.__setattr__(self, "_e", e)
                object.__setattr__(self, "last", None)

            def __getattr__(self, name):
                fn = getattr(self._e, name)
                if callable(fn) and name not in ("wait_ge", "nop", "then_inc"):
                    def wrap(*a, **kw):
                        r = fn(*a, **kw)
                        if hasattr(r, "then_inc"):
                            object.__setattr__(self, "last", r)
                        return r
                    return wrap
                return fn

            def fin(self, sem, k=1):
                self.last.then_inc(sem, k)

        P_ID = nc.partition_id()
        PD2 = [P_ID * 2 + 0, P_ID * 2 + 1]

        def body(engine, fn):
            getattr(blk_ctx, engine)(lambda eng: fn(EngTrack(eng)))

        def f(e):
            loads = [
                (c_cw.ap().rearrange("p (a k) -> p a k", k=KC),
                 cw_in.rearrange("a p k -> p a k")),
                (c_cb[:], cb_in),
                (c_tmp.ap().rearrange("p (a e) -> p a e", e=160),
                 xw_in.rearrange("a p e -> p a e")[:, 0:2]),
                (c_dtb[:], dtb_in),
                (c_A.ap().rearrange("p (a n) -> p a n", n=NST),
                 alog_in.rearrange("a p n -> p a n")),
                (c_dsk[:], dsk_in),
                (c_idf[:], idf_in),
                (c_idb[:], idb_in),
                (c_ones[:], ones_in),
                (c_onesc[:], onesc_in),
                (c_onesr[:], onesr_in),
                (c_onescb[:], onescb_in),
                (c_eps[:], eps_in),
                (q_scs.ap().rearrange("p (a g) -> p a g", g=16),
                 gsc_in.rearrange("a p g -> p a g")),
                (q_dss.ap().rearrange("p (a g) -> p a g", g=44),
                 dsc_in.rearrange("a p g -> p a g")),
                (m_sp[:, :DSL], dtw_in),
            ]
            for dst, src in loads:
                e.dma_start(out=dst, in_=src).then_inc(cld.bump(16), 16)
        body("sync", f)
        CLD0 = cld.n

        def f(e):
            e.wait_ge(cld.h, CLD0)
            e.activation(c_xwb[:, 0:320], c_tmp[:], AF.Copy)
            e.activation(c_dtwb[:], m_sp[:, :DSL], AF.Copy)
            e.activation(c_A[:], c_A[:], AF.Exp)
            e.activation(q_gsc[:], q_scs[:], AF.Copy)
            e.activation(q_dsc[:], q_dss[:], AF.Copy).then_inc(cprep.bump(), 1)
        body("scalar", f)

        for pair in range(1, 8):
            wtmp = cprep.n

            def f(e, pair=pair, wtmp=wtmp):
                e.wait_ge(cprep.h, wtmp)
                e.dma_start(out=c_tmp.ap().rearrange("p (a e) -> p a e", e=160),
                            in_=xw_in.rearrange("a p e -> p a e")
                            [:, 2 * pair:2 * pair + 2]).then_inc(cld.bump(16), 16)
            body("sync", f)
            wld = cld.n

            def f(e, pair=pair, wld=wld):
                e.wait_ge(cld.h, wld)
                e.activation(c_xwb[:, 320 * pair:320 * (pair + 1)], c_tmp[:],
                             AF.Copy).then_inc(cprep.bump(), 1)
            body("scalar", f)

        def f(e, w=cprep.n):
            e.wait_ge(cprep.h, w)
            e.tensor_single_scalar(c_A[:], c_A[:], -1.0, OP.mult)
            e.tensor_single_scalar(q_gsc[:], q_gsc[:], BIG, OP.mult)
            e.tensor_single_scalar(q_dsc[:], q_dsc[:], BIG, OP.mult)
            e.memset(state[:], 0.0).then_inc(cprep.bump(), 1)
        body("vector", f)
        CPREP = cprep.n

        for blk in range(NBLK if phases >= 1 else 0):
            w_prev_tr = xtr.n

            def f(e, blk=blk, w=w_prev_tr):
                if blk >= 1:
                    e.wait_ge(xtr.h, w)
                e.dma_start(out=p_xtm.rearrange("p (a d) -> p a d", d=D),
                            in_=x_blk[blk].rearrange("(a p) d -> p a d", p=128)
                            ).then_inc(xld.bump(16), 16)
            body("sync", f)
            W_XLD = xld.n
            w_prev_cv = cv.n
            w_prev_xsd = xsd.n
            ev_base = xev.n

            def f(e, blk=blk, W_XLD=W_XLD, w_cv=w_prev_cv, ev_base=ev_base):
                if prelvl < 2:
                    return
                e.wait_ge(xld.h, W_XLD)
                e.wait_ge(cld.h, CLD0)
                if blk >= 1:
                    e.wait_ge(cv.h, w_cv)
                for g in range(24):
                    if g >= 1:
                        e.wait_ge(xev.h, ev_base + g - 1)
                    for i in range(2):
                        idx = g * 2 + i
                        a, dc = idx % 3, idx // 3
                        slot = (g % 2) * 2 + i
                        e.transpose(pMix[:, slot * 512:slot * 512 + 128],
                                    p_xtm[:, a * D + dc * 128: a * D + (dc + 1) * 128],
                                    c_idf[:])
                    e.fin(xtr.bump(), 1)
            body("tensor", f)
            tr_base = xtr.n - 24

            def f(e, blk=blk, tr_base=tr_base, w_xsd=w_prev_xsd):
                if prelvl < 2:
                    return
                if blk >= 1:
                    e.wait_ge(xsd.h, w_xsd)
                for g in range(24):
                    e.wait_ge(xtr.h, tr_base + g + 1)
                    for i in range(2):
                        idx = g * 2 + i
                        a, dc = idx % 3, idx // 3
                        slot = (g % 2) * 2 + i
                        e.activation(p_xfm[:, dc * 384 + a * 128: dc * 384 + (a + 1) * 128],
                                     pMix[:, slot * 512:slot * 512 + 128], AF.Copy)
                    e.fin(xev.bump(), 1)
            body("scalar", f)
            W_XEV = xev.n
            w_prev_ud = ud.n

            def f(e, blk=blk, W_XEV=W_XEV, w_ud=w_prev_ud):
                if prelvl < 3:
                    return
                e.wait_ge(xev.h, W_XEV)
                e.wait_ge(cprep.h, CPREP)
                if blk >= 1:
                    e.wait_ge(ud.h, w_ud)
                for dc in range(16):
                    xo = dc * 384
                    e.tensor_scalar(p_csc, p_xfm[:, xo:xo + TBLK],
                                    c_cw[:, dc * KC: dc * KC + 1],
                                    c_cb[:, dc:dc + 1], OP.mult, OP.add)
                    for j in (1, 2):
                        e.scalar_tensor_tensor(p_csc, p_xfm[:, xo + j:xo + j + TBLK],
                                               c_cw[:, dc * KC + j: dc * KC + j + 1],
                                               p_csc, OP.mult, OP.add)
                    e.scalar_tensor_tensor(p_ub[:, dc * TBLK:(dc + 1) * TBLK],
                                           p_xfm[:, xo + 3:xo + 3 + TBLK],
                                           c_cw[:, dc * KC + 3: dc * KC + 4],
                                           p_csc, OP.mult, OP.add)
                e.fin(cv.bump(), 1)
            body("vector", f)
            W_CV = cv.n
            w_prev_xp = xp.n

            def f(e, blk=blk, W_CV=W_CV, w_xp=w_prev_xp):
                if prelvl < 4:
                    return
                e.wait_ge(cv.h, W_CV)
                if blk >= 1:
                    e.wait_ge(xp.h, w_xp)
                for dc in range(16):
                    e.matmul(pBh[:, :TBLK], c_xwb[:, dc * 160: dc * 160 + 128],
                             p_ub[:, dc * TBLK:(dc + 1) * TBLK],
                             start=(dc == 0), stop=(dc == 15))
                for dc in range(16):
                    e.matmul(pCh[:32, :TBLK], c_xwb[:, dc * 160 + 128: dc * 160 + 160],
                             p_ub[:, dc * TBLK:(dc + 1) * TBLK],
                             start=(dc == 0), stop=(dc == 15))
                e.fin(xpe.bump(), 1)
            body("tensor", f)
            W_XPE = xpe.n

            def f(e, blk=blk, W_XPE=W_XPE):
                if prelvl < 4:
                    return
                e.wait_ge(xpe.h, W_XPE)
                e.activation(p_xdbl[:, blk * TBLK:(blk + 1) * TBLK], pBh[:, :TBLK], AF.Copy)
                e.activation(p_xbc[:32, blk * TBLK:(blk + 1) * TBLK], pCh[:32, :TBLK],
                             AF.Copy).then_inc(xp.bump(), 1)
                e.wait_ge(xp.h, xp.n)
                e.dma_start(out=XDBL_b[blk].ap()[0:128],
                            in_=p_xdbl[:, blk * TBLK:(blk + 1) * TBLK]
                            ).then_inc(xdd.bump(16), 16)
                e.dma_start(out=XDBL_b[blk].ap()[128:160],
                            in_=p_xbc[0:32, blk * TBLK:(blk + 1) * TBLK]
                            ).then_inc(xdd.bump(16), 16)
            body("scalar", f)

            def f(e, blk=blk, W_CV=W_CV, W_XEV=W_XEV, W_UD=ud.n + 16):
                if prelvl < 5:
                    return
                e.wait_ge(cv.h, W_CV)
                e.dma_start(out=U_b[blk].ap().rearrange("(dc p) t -> p dc t", p=128),
                            in_=p_ub.rearrange("p (dc t) -> p dc t", t=TBLK)
                            ).then_inc(ud.bump(16), 16)
                e.wait_ge(xev.h, W_XEV)
                e.dma_start(out=XSP[blk].rearrange("(dc p) t -> p dc t", p=128),
                            in_=p_xfm.rearrange("p (dc c) -> p dc c", c=384)
                            [:, :, 3:3 + TBLK]).then_inc(xsd.bump(16), 16)
                e.wait_ge(ud.h, W_UD)
                e.collective_compute("AllGather", OP.bypass,
                                     replica_groups=[list(range(8))],
                                     ins=[U_b[blk][:]], outs=[U_shb[blk][:]]
                                     ).then_inc(ag.bump(), 1)
                e.wait_ge(xdd.h, 32 * (blk + 1))
                e.collective_compute("AllGather", OP.bypass,
                                     replica_groups=[list(range(8))],
                                     ins=[XDBL_b[blk][:]], outs=[XDBL_shb[blk][:]]
                                     ).then_inc(ag.bump(), 1)
            body("gpsimd", f)

        W_XDD = xdd.n

        def emit_dequant(words, hgrp, sc_tile, sc_off, src_ap, dst_rows):
            nfree = words * 8
            w_qsh = qsh.n

            first = DEQ_GUARDS.pop("first", False)

            def f(e, w=w_qsh, first=first):
                if first:
                    e.wait_ge(xtr.h, DEQ_GUARDS["xtr"])
                e.wait_ge(qsh.h, w)
                e.dma_start(out=q_pk[:, :words], in_=src_ap).then_inc(qld.bump(16), 16)
            body("sync", f)
            w_qld = qld.n
            w_qtr_prev = qtr.n

            def f(e, w_qld=w_qld, words=words, nfree=nfree, hgrp=hgrp,
                  sc_off=sc_off, sc_tile=sc_tile, w_qtr=w_qtr_prev, first=first):
                e.wait_ge(qld.h, w_qld)
                e.wait_ge(cprep.h, CPREP)
                if first:
                    e.wait_ge(xdd.h, DEQ_GUARDS["xdd"])
                    e.wait_ge(ud.h, DEQ_GUARDS["ud"])
                    e.wait_ge(xsd.h, DEQ_GUARDS["xsd"])
                for j in range(8):
                    e.tensor_single_scalar(q_t0[:, j:nfree:8], q_pk[:, :words],
                                           4 * j, OP.logical_shift_right)
                e.fin(qsh.bump(), 1)
                e.wait_ge(qtr.h, w_qtr)
                e.tensor_scalar(q_bits[:, :nfree], q_t0[:, :nfree], 7, 22,
                                OP.bitwise_and, OP.logical_shift_left)
                e.tensor_scalar(q_t0[:, :nfree], q_t0[:, :nfree], 8, 28,
                                OP.bitwise_and, OP.logical_shift_left)
                e.tensor_tensor(q_bits[:, :nfree], q_bits[:, :nfree],
                                q_t0[:, :nfree], OP.bitwise_or)
                bf = q_bits.bitcast(F32)
                for g in range(hgrp):
                    e.tensor_scalar(q_deq[:, g * 128:(g + 1) * 128],
                                    bf[:, g * 128:(g + 1) * 128],
                                    sc_tile[:, sc_off + g: sc_off + g + 1], None, OP.mult)
                e.fin(qdq.bump(), 1)
            body("vector", f)
            w_qdq = qdq.n
            ngr = (hgrp + 1) // 2
            trb, evb = qtr.n, qev.n
            w_qwd_prev = qwd.n

            def f(e, w_qdq=w_qdq, hgrp=hgrp, ngr=ngr, trb=trb, evb=evb,
                  w_qwd=w_qwd_prev):
                e.wait_ge(qdq.h, w_qdq)
                e.wait_ge(qwd.h, w_qwd)
                for g in range(ngr):
                    if g >= 1:
                        e.wait_ge(qev.h, evb + g - 1)
                    for i in range(min(2, hgrp - g * 2)):
                        gg = g * 2 + i
                        slot = (g % 2) * 2 + i
                        e.transpose(pMix[:, slot * 512:slot * 512 + 128],
                                    q_deq[:, gg * 128:(gg + 1) * 128], c_idf[:])
                    e.fin(qtr.bump(), 1)
            body("tensor", f)

            def f(e, hgrp=hgrp, ngr=ngr, trb=trb):
                for g in range(ngr):
                    e.wait_ge(qtr.h, trb + g + 1)
                    for i in range(min(2, hgrp - g * 2)):
                        gg = g * 2 + i
                        slot = (g % 2) * 2 + i
                        e.activation(q_wt[:, gg * 128:(gg + 1) * 128],
                                     pMix[:, slot * 512:slot * 512 + 128], AF.Copy)
                    e.fin(qev.bump(), 1)
            body("scalar", f)
            w_qev = qev.n

            def f(e, w_qev=w_qev, dst_ap=dst_rows, nfree=nfree):
                e.wait_ge(qev.h, w_qev)
                e.dma_start(out=dst_ap,
                            in_=q_wt[:, 0:nfree].rearrange("p (ic c) -> p ic c",
                                                           c=128)
                            ).then_inc(qwd.bump(16), 16)
            body("gpsimd", f)

        DEQ_GUARDS = dict(first=True, xtr=xtr.n, xdd=xdd.n, ud=ud.n, xsd=xsd.n,
                          xpe=xpe.n)
        for oc in range(11 if phases >= 2 else 0):
            emit_dequant(256, 16, q_gsc, oc * 16,
                         gup_in.rearrange("(a p) w -> a p w", p=128)[oc],
                         W_gu_cc.ap().rearrange("p (t c) -> p t c", c=128)
                         [:, oc * 16:(oc + 1) * 16])
        W_QWD_GU = qwd.n
        for hc2 in range(2 if phases >= 2 else 0):
            emit_dequant(704, 44, q_dsc, hc2 * 44,
                         dnp_in.rearrange("(a p) w -> a p w", p=128)[hc2],
                         W_dn_cc.ap().rearrange("p (ic two c) -> p ic two c",
                                                two=2, c=128)[:, :, hc2])
        W_QEV_ALL = qev.n
        W_QWD_ALL = qwd.n

        AG_ULAST = ag.n

        def f(e, wg=W_QWD_GU, ww=W_QWD_ALL):
            if phases < 3:
                return
            e.wait_ge(qwd.h, wg)
            e.collective_compute("AllGather", OP.bypass, replica_groups=[list(range(8))],
                                 ins=[W_gu_cc[:]], outs=[W_gu_sh[:]]).then_inc(ag.bump(), 1)
            e.wait_ge(qwd.h, ww)
            e.collective_compute("AllGather", OP.bypass, replica_groups=[list(range(8))],
                                 ins=[W_dn_cc[:]], outs=[W_dn_sh[:]]).then_inc(ag.bump(), 1)
        body("gpsimd", f)
        AG_WGU = AG_ULAST + 1
        AG_WDN = AG_ULAST + 2
        AG_STAGE = [2 * (4 + k) + 2 for k in range(NSTAGE)]

        XDVB = [XDBL_shb[l].ap().rearrange("(r e) t -> r e t", r=8)
                for l in range(NBLK)]

        NK = 0 if phases < 4 else (1 if phases == 4 else NSTAGE)
        for k in range(NK):
            w_mdtb_prev = mdtb.n
            w_mze_prev = mze.n
            w_mda_prev = mda.n

            w_fod_mix = fod.n

            def f(e, k=k, w1=w_mdtb_prev, w2=w_mze_prev, w3=w_fod_mix):
                e.wait_ge(ag.h, AG_STAGE[k])
                if k == 0:
                    e.wait_ge(qwd.h, W_QWD_ALL)
                else:
                    e.wait_ge(mdtb.h, w1)
                    e.wait_ge(mze.h, w2)
                    if w3 > 0:
                        e.wait_ge(fod.h, w3)
                for b in range(B):
                    e.dma_start(out=m_dt[:, b * TSTG:(b + 1) * TSTG]
                                .rearrange("p (j t) -> p j t", t=TBLK),
                                in_=XDVB[b * 4 + k][:, 0:128, :].transpose([1, 0, 2])
                                ).then_inc(mld.bump(16), 16)
                    for dblk in range(2):
                        usrc = U_shb[b * 4 + k].ap().rearrange(
                            "(r dc p) t -> dc r p t", r=8, p=128)
                        e.dma_start(out=m_u[:, (b * 2 + dblk) * TSTG:
                                                 (b * 2 + dblk + 1) * TSTG]
                                    .rearrange("p (j t) -> p j t", t=TBLK),
                                    in_=usrc[bass.ts(PD2[dblk], 1)][0]
                                    .transpose([1, 0, 2])).then_inc(mld.bump(16), 16)
            body("sync", f)
            W_MLD = mld.n

            w_trig_prev = TRIG_DONE.get(k - 2)

            def f(e, k=k, w=w_trig_prev):
                e.wait_ge(ag.h, AG_STAGE[k])
                if w is not None:
                    e.wait_ge(mbbP[0].h, w[0])
                    e.wait_ge(mbbP[1].h, w[1])
                    e.wait_ge(mcbP[0].h, w[2])
                    e.wait_ge(mcbP[1].h, w[3])
                for b in range(B):
                    e.dma_start(out=BC_stage.ap()[k % 2, b].rearrange(
                                    "e (j t) -> e j t", t=TBLK),
                                in_=XDVB[b * 4 + k][:, 128:160, :]
                                .transpose([1, 0, 2])).then_inc(mrw.bump(16), 16)
            body("scalar", f)
            W_ROWS = mrw.n

            def f(e, w=W_MLD):
                e.wait_ge(mld.h, w)
                e.activation(m_dtb, m_dt, AF.Copy).then_inc(mdtb.bump(), 1)
            body("scalar", f)
            W_MDTB = mdtb.n

            if k == 0 and ("mu" in dbg or "mdt" in dbg):
                def f(e, w=W_MLD):
                    e.wait_ge(mld.h, w)
                    if "mu" in dbg:
                        e.dma_start(out=dbg["mu"], in_=m_u).then_inc(dbgs.bump(16), 16)
                    if "mdt" in dbg:
                        e.dma_start(out=dbg["mdt"], in_=m_dt).then_inc(dbgs.bump(16), 16)
                body("scalar", f)

            for b in range(B):
                for dblk in range(2):
                    bd = b * 2 + dblk
                    w_msp_prev = msp.n

                    w_mze_dt = mze.n

                    def f(e, b=b, dblk=dblk, k=k, w1=W_MDTB, w2=w_msp_prev,
                          wq=W_QEV_ALL, w3=w_mze_dt):
                        e.wait_ge(mdtb.h, w1)
                        e.wait_ge(qev.h, wq)
                        if w3 > 0:
                            e.wait_ge(mze.h, w3)
                        if not (k == 0 and b == 0 and dblk == 0):
                            e.wait_ge(msp.h, w2)
                        for q in range(4):
                            e.matmul(pMix[:, q * 512:(q + 1) * 512],
                                     c_dtwb[:, dblk * 128:(dblk + 1) * 128],
                                     m_dtb[:, b * TSTG + q * 512: b * TSTG + (q + 1) * 512],
                                     start=True, stop=True)
                        e.fin(mdp.bump(), 1)
                    body("tensor", f)
                    W_MDP = mdp.n

                    w_mdu_sp = mdu.n

                    def f(e, dblk=dblk, w=W_MDP, wdu=w_mdu_sp):
                        e.wait_ge(mdp.h, w)
                        if wdu > 0:
                            e.wait_ge(mdu.h, wdu)
                        for q in range(4):
                            sl = slice(q * 512, (q + 1) * 512)
                            psl = pMix[:, q * 512:(q + 1) * 512]
                            bias = c_dtb[:, dblk:dblk + 1]
                            e.activation(m_sp[:, sl], psl, AF.Abs, bias=bias)
                            e.activation(m_sp[:, sl], m_sp[:, sl], AF.Exp, scale=-1.0)
                            e.activation(m_sp[:, sl], m_sp[:, sl], AF.Ln, bias=1.0)
                            e.activation(m_rel[:, sl], psl, AF.Relu, bias=bias)
                        e.fin(msp.bump(), 1)
                    body("scalar", f)
                    W_MSP = msp.n

                    def f(e, bd=bd, k=k, w=W_MSP, wda=w_mda_prev,
                          wld=W_MLD):
                        e.wait_ge(msp.h, w)
                        e.wait_ge(mld.h, wld)
                        if k >= 1:
                            e.wait_ge(mda.h, wda)
                        e.tensor_tensor(m_delta[:, bd * TSTG:(bd + 1) * TSTG],
                                        m_sp, m_rel, OP.add)
                        e.tensor_tensor(m_du[:, bd * TSTG:(bd + 1) * TSTG],
                                        m_delta[:, bd * TSTG:(bd + 1) * TSTG],
                                        m_u[:, bd * TSTG:(bd + 1) * TSTG],
                                        OP.mult).then_inc(mdu.bump(), 1)
                    body("vector", f)
            W_MDU = mdu.n

            if k == 0 and "delta" in dbg:
                def f(e, w=W_MDU):
                    e.wait_ge(mdu.h, w)
                    e.dma_start(out=dbg["delta"], in_=m_delta).then_inc(dbgs.bump(16), 16)
                body("scalar", f)

            for b in range(B):
                for dblk in range(2):
                    bd = b * 2 + dblk
                    dsl = slice(bd * TSTG, (bd + 1) * TSTG)
                    base_da = mda.n
                    base_sc = msc.n
                    base_bb = (mbbP[0].n, mbbP[1].n)
                    base_cb = (mcbP[0].n, mcbP[1].n)
                    base_hc = (mhcP[0].n, mhcP[1].n)
                    base_yq = myq.n
                    base_sv = msv.n
                    base_bu = mbu.n
                    w_mze_u = mze.n
                    w_mzd_u = mzd.n

                    def f(e, b=b, k=k, w=W_MDU, W_ROWS=W_ROWS,
                          base_sc=base_sc, base_hc=base_hc):
                        e.wait_ge(mrw.h, W_ROWS)
                        e.wait_ge(mdu.h, w)
                        for n in range(NST):
                            wv = base_sc + max(0, n - 1)
                            if wv > 0:
                                e.wait_ge(msc.h, wv)
                            e.dma_start(
                                out=m_dBu[:, (n % 2) * TSTG:(n % 2 + 1) * TSTG],
                                in_=BC_stage.ap()[k % 2, b, n:n + 1, :]
                                .partition_broadcast(128)[:, 0]
                                ).then_inc(mbbP[n % 2].bump(16), 16)
                            wv = base_hc[n % 2] + (n // 2 if n >= 2 else 0)
                            if wv > 0:
                                e.wait_ge(mhcP[n % 2].h, wv)
                            e.dma_start(
                                out=m_Cb2[n % 2],
                                in_=BC_stage.ap()[k % 2, b, 16 + n:17 + n, :]
                                .partition_broadcast(128)[:, 0]
                                ).then_inc(mcbP[n % 2].bump(16), 16)
                    body("sync", f)

                    def f(e, dblk=dblk, dsl=dsl, bd=bd, w=W_MDU,
                          base_sc=base_sc, base_da=base_da):
                        e.wait_ge(mdu.h, w)
                        for n in range(NST):
                            if base_sc + n - 1 > 0:
                                e.wait_ge(msc.h, base_sc + n - 1)
                            e.activation(m_dA[:, (n % 2) * TSTG:(n % 2 + 1) * TSTG],
                                         m_delta[:, dsl], AF.Exp,
                                         scale=c_A[:, dblk * NST + n: dblk * NST + n + 1]
                                         ).then_inc(mda.bump(), 1)
                            if n >= 1:
                                e.wait_ge(msc.h, base_sc + n)
                                e.activation(state[:, bd * NST + n - 1: bd * NST + n],
                                             m_h2[(n - 1) % 2][:, TSTG - 1:TSTG],
                                             AF.Copy).then_inc(msv.bump(), 1)
                        e.wait_ge(msc.h, base_sc + NST)
                        e.activation(state[:, bd * NST + NST - 1: bd * NST + NST],
                                     m_h2[(NST - 1) % 2][:, TSTG - 1:TSTG],
                                     AF.Copy).then_inc(msv.bump(), 1)
                    body("scalar", f)

                    KGPS = int(os.environ.get("KGPS", "1"))
                    if KGPS:
                        def f(e, dsl=dsl, w=W_MDU, base_bb=base_bb,
                              base_cb=base_cb, base_sc=base_sc, base_yq=base_yq):
                            e.wait_ge(mdu.h, w)

                            def hC(j):
                                e.wait_ge(msc.h, base_sc + j + 1)
                                e.wait_ge(mcbP[1].h, base_cb[1] + 16 * (j // 2 + 1))
                                if base_yq + j - 1 > 0:
                                    e.wait_ge(myq.h, base_yq + j - 1)
                                e.tensor_tensor(
                                    m_hC[:, TSTG:2 * TSTG],
                                    m_h2[1], m_Cb2[1],
                                    OP.mult).then_inc(mhcP[1].bump(), 1)
                            for n in range(NST):
                                e.wait_ge(mbbP[n % 2].h,
                                          base_bb[n % 2] + 16 * (n // 2 + 1))
                                e.tensor_tensor(m_dBu2[n % 2],
                                                m_du[:, dsl],
                                                m_dBu[:, (n % 2) * TSTG:(n % 2 + 1) * TSTG],
                                                OP.mult).then_inc(mbu.bump(), 1)
                                if n >= 2 and n % 2 == 0:
                                    hC(n - 1)
                            hC(NST - 1)
                        body("gpsimd", f)

                    def f(e, bd=bd, dblk=dblk, dsl=dsl, base_bu=base_bu,
                          base_da=base_da, base_sv=base_sv, base_hc=base_hc,
                          base_yq=base_yq, base_cb=base_cb, base_bb=base_bb,
                          w_mzd_u=w_mzd_u, KGPS=KGPS):
                        for n in range(NST):
                            par = n % 2
                            if KGPS:
                                e.wait_ge(mbu.h, base_bu + n + 1)
                            else:
                                e.wait_ge(mbbP[par].h,
                                          base_bb[par] + 16 * (n // 2 + 1))
                                e.tensor_tensor(m_dBu2[par], m_du[:, dsl],
                                                m_dBu[:, par * TSTG:(par + 1) * TSTG],
                                                OP.mult).then_inc(mbu.bump(), 1)
                            e.wait_ge(mda.h, base_da + n + 1)
                            wv = base_sv + max(0, n - 1)
                            if wv > 0:
                                e.wait_ge(msv.h, wv)
                            wv = base_hc[par] + (n // 2 if n >= 2 else 0)
                            if wv > 0:
                                e.wait_ge(mhcP[par].h, wv)
                            scol = state[:, bd * NST + n: bd * NST + n + 1]
                            e.tensor_tensor_scan(m_h2[par],
                                                 m_dA[:, par * TSTG:(par + 1) * TSTG],
                                                 m_dBu2[par],
                                                 scol, OP.mult,
                                                 OP.add).then_inc(msc.bump(), 1)
                            if par == 0 or not KGPS:
                                e.wait_ge(mcbP[par].h,
                                          base_cb[par] + 16 * (n // 2 + 1))
                                if base_yq + n - 1 > 0:
                                    e.wait_ge(myq.h, base_yq + n - 1)
                                e.tensor_tensor(m_hC[:, par * TSTG:(par + 1) * TSTG],
                                                m_h2[par], m_Cb2[par],
                                                OP.mult).then_inc(mhcP[par].bump(), 1)
                        e.wait_ge(myq.h, base_yq + NST)
                        if w_mzd_u > 0:
                            e.wait_ge(mzd.h, w_mzd_u)
                        e.tensor_scalar(m_z, m_u[:, dsl],
                                        c_dsk[:, dblk:dblk + 1], None, OP.mult)
                        e.tensor_tensor(m_z, pMix[:], m_z,
                                        OP.add).then_inc(mze.bump(), 1)
                    body("vector", f)

                    def f(e, bd=bd, k=k, wze=w_mze_u, base_hc=base_hc):
                        if not (k == 0 and bd == 0):
                            e.wait_ge(mze.h, wze)
                        for n in range(NST):
                            e.wait_ge(mhcP[n % 2].h, base_hc[n % 2] + n // 2 + 1)
                            hcb = (n % 2) * TSTG
                            for q in range(4):
                                e.matmul(pMix[:, q * 512:(q + 1) * 512], c_idb[:],
                                         m_hC[:, hcb + q * 512: hcb + (q + 1) * 512],
                                         start=(n == 0), stop=(n == NST - 1))
                            e.fin(myq.bump(), 1)
                    body("tensor", f)
                    W_MZE = mze.n

                    def f(e, k=k, bd=bd, w=W_MZE, wzd=mzd.n + 16):
                        e.wait_ge(mze.h, w)
                        e.dma_start(out=Z_ccu[k][bd][:],
                                    in_=m_z).then_inc(mzd.bump(16), 16)
                        e.wait_ge(mzd.h, wzd)
                        e.collective_compute("AllGather", OP.bypass,
                                             replica_groups=[list(range(8))],
                                             ins=[Z_ccu[k][bd][:]],
                                             outs=[Z_shu[k][bd][:]]
                                             ).then_inc(ag.bump(), 1)
                    body("gpsimd", f)

            TRIG_DONE[k] = (mbbP[0].n, mbbP[1].n, mcbP[0].n, mcbP[1].n)
            W_MZD = mzd.n
            AG_ZK = ag.n

            if "zz" in dbg:
                def f(e, k=k, w=W_MZD):
                    e.wait_ge(mzd.h, w)
                    for u in range(4):
                        e.dma_start(out=dbg["zz"].rearrange(
                                        "(kk u p) t -> kk u p t", kk=4, u=4)[k, u],
                                    in_=Z_ccu[k][u][:]).then_inc(dbgs.bump(16), 16)
                body("scalar", f)

            if phases < 6:
                continue
            w_fod_prev = fod.n

            def f(e, k=k, wagz=AG_ZK, wfod=w_fod_prev):
                e.wait_ge(ag.h, wagz)
                for b in range(B):
                    for db in range(2):
                        zsrc = Z_shu[k][b * 2 + db].ap().rearrange(
                            "(r p) t -> r p t", p=128)
                        e.dma_start(
                            out=f_bigA.rearrange("p (dc b2 t) -> p dc b2 t",
                                                 b2=B, t=TBLK)[:, db::2, b],
                            in_=zsrc[:, :, bass.ts(P_ID, TBLK)].transpose([1, 0, 2])
                            ).then_inc(fzl.bump(16), 16)
                if wfod > 0:
                    e.wait_ge(fod.h, wfod)
                e.dma_start(out=f_out2.rearrange("p (dc t) -> p dc t", t=TBLK),
                            in_=XSP[4 * 0 + k].rearrange("(dc p) t -> p dc t", p=128)
                            ).then_inc(fxl.bump(16), 16)
            body("sync", f)
            W_FZL, W_FXL0 = fzl.n, fxl.n

            def f(e, w1=W_FZL, w2=W_FXL0):
                e.wait_ge(fzl.h, w1)
                e.wait_ge(fxl.h, w2)
                e.tensor_tensor(f_bigA.rearrange("p (dc b2 t) -> p dc b2 t",
                                                 b2=B, t=TBLK)[:, :, 0],
                                f_bigA.rearrange("p (dc b2 t) -> p dc b2 t",
                                                 b2=B, t=TBLK)[:, :, 0],
                                f_out2.rearrange("p (dc t) -> p dc t", t=TBLK),
                                OP.add).then_inc(fr1.bump(), 1)
            body("vector", f)
            w_fr1_b0 = fr1.n

            def f(e, k=k, w=w_fr1_b0):
                e.wait_ge(fr1.h, w)
                e.dma_start(out=f_out2.rearrange("p (dc t) -> p dc t", t=TBLK),
                            in_=XSP[4 * 1 + k].rearrange("(dc p) t -> p dc t", p=128)
                            ).then_inc(fxl.bump(16), 16)
            body("sync", f)
            W_FXL1 = fxl.n

            def f(e, w2=W_FXL1):
                e.wait_ge(fxl.h, w2)
                e.tensor_tensor(f_bigA.rearrange("p (dc b2 t) -> p dc b2 t",
                                                 b2=B, t=TBLK)[:, :, 1],
                                f_bigA.rearrange("p (dc b2 t) -> p dc b2 t",
                                                 b2=B, t=TBLK)[:, :, 1],
                                f_out2.rearrange("p (dc t) -> p dc t", t=TBLK),
                                OP.add).then_inc(fr1.bump(), 1)
            body("vector", f)
            W_FR1 = fr1.n

            def f(e, w=W_FR1):
                e.wait_ge(fr1.h, w)
                e.activation(f_x1b, f_bigA, AF.Copy)
                e.activation(f_r1sq, f_bigA, AF.Square).then_inc(fmq.bump(), 1)
            body("scalar", f)
            W_FMQ = fmq.n

            def f(e, w=W_FMQ):
                e.wait_ge(fmq.h, w)
                for dc in range(16):
                    e.matmul(pBh[0:1, 0:512], c_onescb[:],
                             f_x1b[:, dc * 512:(dc + 1) * 512],
                             start=(dc == 0), stop=(dc == 15))
                for dc in range(16):
                    e.matmul(pBh[0:1, 512:1024], c_onescb[:],
                             f_r1sq[:, dc * 512:(dc + 1) * 512],
                             start=(dc == 0), stop=(dc == 15))
                e.fin(fmo.bump(), 1)
            body("tensor", f)
            W_FMO = fmo.n

            if k == 0 and "ya" in dbg:
                def f(e, w=W_FMO):
                    e.wait_ge(fmo.h, w)
                    e.dma_start(out=dbg["ya"], in_=f_r1sq[:, 0:512]).then_inc(dbgs.bump(16), 16)
                body("gpsimd", f)

            def f(e, w=W_FMO):
                e.wait_ge(fmo.h, w)
                e.activation(f_rows[:, 0:512], pBh[0:1, 0:512], AF.Copy, scale=1.0 / D)
                e.activation(f_rows[:, 512:1024], pBh[0:1, 512:1024], AF.Copy,
                             scale=1.0 / D)
                e.activation(f_rows[:, 1024:1536], f_rows[:, 0:512],
                             AF.Square).then_inc(frw.bump(), 1)
            body("scalar", f)

            def f(e, w=frw.n):
                e.wait_ge(frw.h, w)
                e.tensor_tensor(f_rows[:, 1024:1536], f_rows[:, 512:1024],
                                f_rows[:, 1024:1536], OP.subtract).then_inc(frw.bump(), 1)
            body("vector", f)

            def f(e, w=frw.n):
                e.wait_ge(frw.h, w)
                e.activation(f_rows[:, 1536:2048], f_rows[:, 1024:1536], AF.Ln,
                             bias=c_eps.ap()[0:1, 0:1])
                e.activation(f_rows[:, 1536:2048], f_rows[:, 1536:2048], AF.Exp,
                             scale=-0.5).then_inc(frw.bump(), 1)
            body("scalar", f)

            def f(e, w=frw.n):
                e.wait_ge(frw.h, w)
                e.scalar_tensor_tensor(f_rows[:, 2048:2560], f_rows[:, 0:512], -1.0,
                                       f_rows[:, 1536:2048], OP.mult,
                                       OP.mult).then_inc(frw.bump(), 1)
            body("vector", f)
            W_FRW = frw.n

            def f(e, w=W_FRW):
                e.wait_ge(frw.h, w)
                e.matmul(pMix[:, 1024:1536], c_onesr[:], f_rows[:, 1536:2048],
                         start=True, stop=True)
                e.matmul(pMix[:, 1536:2048], c_onesr[:], f_rows[:, 2048:2560],
                         start=True, stop=True)
                e.fin(fbc.bump(), 1)
            body("tensor", f)
            W_FBC = fbc.n

            def f(e, w=W_FBC):
                e.wait_ge(fbc.h, w)
                for dc in range(16):
                    sl = slice(dc * 512, (dc + 1) * 512)
                    e.tensor_tensor(f_x1t, f_bigA[:, sl],
                                    pMix[:, 1024:1536], OP.mult)
                    e.tensor_tensor(f_bigA[:, sl], f_x1t,
                                    pMix[:, 1536:2048], OP.add)
                e.fin(fx1.bump(), 1)
            body("vector", f)
            W_FX1 = fx1.n

            def f(e, w=W_FX1):
                e.wait_ge(fx1.h, w)
                e.activation(f_x1b, f_bigA, AF.Copy).then_inc(fxb.bump(), 1)
            body("scalar", f)
            W_FXB = fxb.n

            if k == 0 and "x1" in dbg:
                def f(e, w=W_FX1):
                    e.wait_ge(fx1.h, w)
                    e.dma_start(out=dbg["x1"], in_=f_bigA).then_inc(dbgs.bump(16), 16)
                body("gpsimd", f)
            if k == 0 and "yb" in dbg:
                def f(e, w=W_FXB):
                    e.wait_ge(fxb.h, w)
                    e.dma_start(out=dbg["yb"], in_=f_r1sq[:, 0:512]).then_inc(dbgs.bump(16), 16)
                body("gpsimd", f)
            trb, teb = ftr.n, fte.n

            def f(e, w=W_FX1, trb=trb, teb=teb):
                e.wait_ge(fx1.h, w)
                for g in range(32):
                    if g >= 1:
                        e.wait_ge(fte.h, teb + g - 1)
                    for i in range(2):
                        idx = g * 2 + i
                        tc, dc = idx // 16, idx % 16
                        slot = (g % 2) * 2 + i
                        e.transpose(pMix[:, slot * 512:slot * 512 + 128],
                                    f_bigA[:, dc * 512 + tc * 128:
                                           dc * 512 + (tc + 1) * 128],
                                    c_idf[:])
                    e.fin(ftr.bump(), 1)
            body("tensor", f)

            def f(e, trb=trb):
                for g in range(32):
                    e.wait_ge(ftr.h, trb + g + 1)
                    for i in range(2):
                        idx = g * 2 + i
                        tc, dc = idx // 16, idx % 16
                        slot = (g % 2) * 2 + i
                        e.activation(f_x1tm[:, tc * D + dc * 128:
                                            tc * D + (dc + 1) * 128],
                                     pMix[:, slot * 512:slot * 512 + 128], AF.Copy)
                    e.fin(fte.bump(), 1)
            body("scalar", f)
            W_FTE = fte.n

            if k == 0 and "yc" in dbg:
                def f(e, w=W_FTE):
                    e.wait_ge(fte.h, w)
                    e.dma_start(out=dbg["yc"], in_=f_r1sq[:, 0:512]).then_inc(dbgs.bump(16), 16)
                body("gpsimd", f)
            if k == 0 and "x1tm" in dbg:
                def f(e, w=W_FTE):
                    e.wait_ge(fte.h, w)
                    e.dma_start(out=dbg["x1tm"], in_=f_x1tm).then_inc(dbgs.bump(16), 16)
                body("gpsimd", f)
            FGU_AT, FSL_AT, FHH_AT = [], [], []
            for ocp in range(44):
                pg = pBh if ocp % 2 == 0 else pCh
                wbuf = (ocp % 2) * 2 * 16 * 128
                w_fgu_m2 = FGU_AT[ocp - 2] if ocp >= 2 else (fgu.n if k > 0 else 0)
                w_fr2_prev = fr2.n if (ocp < 2 and k > 0) else 0

                def f(e, ocp=ocp, wbuf=wbuf, w_fgu=w_fgu_m2, wr2=w_fr2_prev):
                    e.wait_ge(ag.h, AG_WGU)
                    if wr2 > 0:
                        e.wait_ge(fr2.h, wr2)
                    if w_fgu > 0:
                        e.wait_ge(fgu.h, w_fgu)
                    for half, oc in ((0, ocp), (1, 44 + ocp)):
                        r, ocl = oc // 11, oc % 11
                        e.dma_start(
                            out=f_wgu[:, wbuf + half * 16 * 128:
                                           wbuf + (half + 1) * 16 * 128],
                            in_=W_gu_sh.ap()[128 * r:128 * (r + 1),
                                             ocl * 2048:(ocl + 1) * 2048]
                            ).then_inc(fwg.bump(16), 16)
                body("sync", f)
                W_FWG = fwg.n
                w_fhh_m2 = FHH_AT[ocp - 2] if ocp >= 2 else (fhh.n if k > 0 else 0)

                def f(e, ocp=ocp, pg=pg, wbuf=wbuf, w=W_FWG, wxb=W_FXB,
                      w_fhh=w_fhh_m2):
                    e.wait_ge(fwg.h, w)
                    e.wait_ge(fxb.h, wxb)
                    if w_fhh > 0:
                        e.wait_ge(fhh.h, w_fhh)
                    for half in range(2):
                        for hc in range(16):
                            e.matmul(pg[:, half * 512:(half + 1) * 512],
                                     f_wgu[:, wbuf + half * 16 * 128 + hc * 128:
                                           wbuf + half * 16 * 128 + (hc + 1) * 128],
                                     f_x1b[:, hc * 512:(hc + 1) * 512],
                                     start=(hc == 0), stop=(hc == 15))
                    e.fin(fgu.bump(), 1)
                body("tensor", f)
                FGU_AT.append(fgu.n)
                w_fhh_m1 = FHH_AT[ocp - 1] if ocp >= 1 else (fhh.n if k > 0 else 0)

                def f(e, pg=pg, w=fgu.n, w_fhh=w_fhh_m1):
                    e.wait_ge(fgu.h, w)
                    if w_fhh > 0:
                        e.wait_ge(fhh.h, w_fhh)
                    e.activation(f_sg, pg[:, 0:512], AF.Silu).then_inc(fsl.bump(), 1)
                body("scalar", f)
                FSL_AT.append(fsl.n)

                def f(e, pg=pg, ocp=ocp, w=fsl.n):
                    e.wait_ge(fsl.h, w)
                    e.tensor_tensor(f_hh[:, ocp * 512:(ocp + 1) * 512], f_sg,
                                    pg[:, 512:1024], OP.mult).then_inc(fhh.bump(), 1)
                body("vector", f)
                FHH_AT.append(fhh.n)
            W_FHH = fhh.n

            if k == 0 and "yd" in dbg:
                def f(e, w=fhh.n):
                    e.wait_ge(fhh.h, w)
                    e.dma_start(out=dbg["yd"], in_=f_r1sq[:, 0:512]).then_inc(dbgs.bump(16), 16)
                body("gpsimd", f)
            if k == 0 and "hh" in dbg:
                def f(e, w=W_FHH):
                    e.wait_ge(fhh.h, w)
                    e.dma_start(out=dbg["hh"], in_=f_hh).then_inc(dbgs.bump(16), 16)
                body("gpsimd", f)
            for hcc in range(8):
                w_fde_prev = fde.n

                def f(e, hcc=hcc, k=k, w=w_fde_prev):
                    e.wait_ge(ag.h, AG_WDN)
                    if w > 0:
                        e.wait_ge(fde.h, w)
                    e.dma_start(
                        out=f_wdn[:, 0:88 * 128],
                        in_=W_dn_sh.ap()[128 * hcc:128 * (hcc + 1), :]
                        ).then_inc(fwd.bump(16), 16)
                body("sync", f)
                W_FWD = fwd.n

                def f(e, hcc=hcc, w=W_FWD, whh=W_FHH, wde=w_fde_prev, k=k,
                      wte=W_FTE):
                    e.wait_ge(fwd.h, w)
                    e.wait_ge(fhh.h, whh)
                    e.wait_ge(fte.h, wte)
                    if wde > 0:
                        e.wait_ge(fde.h, wde)
                    for tc in range(4):
                        for ic in range(44):
                            e.matmul(pMix[:, tc * 512:tc * 512 + 256],
                                     f_hh[:, ic * 512 + tc * 128:
                                          ic * 512 + (tc + 1) * 128],
                                     f_wdn[:, ic * 256:(ic + 1) * 256],
                                     start=(ic == 0), stop=(ic == 43))
                    e.fin(fdn.bump(), 1)
                body("tensor", f)
                W_FDN = fdn.n

                def f(e, hcc=hcc, w=W_FDN):
                    e.wait_ge(fdn.h, w)
                    for tc in range(4):
                        e.tensor_copy(f_bigA[:, tc * D + hcc * 256:
                                             tc * D + (hcc + 1) * 256],
                                      pMix[:, tc * 512:tc * 512 + 256])
                    e.fin(fde.bump(), 1)
                body("vector", f)
            W_FDE = fde.n

            if k == 0 and "y2f" in dbg:
                def f(e, w=W_FDE):
                    e.wait_ge(fde.h, w)
                    e.dma_start(out=dbg["y2f"], in_=f_bigA).then_inc(dbgs.bump(16), 16)
                body("gpsimd", f)

            def f(e, w=W_FDE):
                e.wait_ge(fde.h, w)
                for tc in range(4):
                    e.activation(f_sqd, f_bigA[:, tc * D:(tc + 1) * D], AF.Square,
                                 accum_out=f_cols[:, 32 + tc:33 + tc])
                e.activation(f_cols[:, 36:40], f_cols[:, 32:36], AF.Ln,
                             scale=1.0 / D, bias=c_eps.ap()[:, 1:2])
                e.activation(f_cols[:, 36:40], f_cols[:, 36:40], AF.Exp,
                             scale=-0.5).then_inc(fc2.bump(), 1)
            body("scalar", f)
            W_FC2A = fc2.n

            def f(e, w=W_FC2A, wte=W_FTE):
                e.wait_ge(fc2.h, w)
                e.wait_ge(fte.h, wte)
                for tc in range(4):
                    sl = slice(tc * D, (tc + 1) * D)
                    e.tensor_scalar(f_bigA[:, sl], f_bigA[:, sl],
                                    f_cols.ap()[:, 36 + tc:37 + tc], None, OP.mult)
                    e.tensor_tensor(f_bigA[:, sl], f_bigA[:, sl],
                                    f_x1tm[:, sl], OP.add)
                for tc in range(4):
                    e.tensor_reduce(f_cols[:, 40 + tc:41 + tc],
                                    f_bigA[:, tc * D:(tc + 1) * D], AX.X, OP.add)
                e.fin(fr2.bump(), 1)
            body("vector", f)
            W_FR2 = fr2.n

            if k == 0 and "r2" in dbg:
                def f(e, w=W_FR2):
                    e.wait_ge(fr2.h, w)
                    e.dma_start(out=dbg["r2"], in_=f_bigA).then_inc(dbgs.bump(16), 16)
                body("gpsimd", f)

            def f(e, w=W_FR2):
                e.wait_ge(fr2.h, w)
                for tc in range(4):
                    e.activation(f_sqd, f_bigA[:, tc * D:(tc + 1) * D], AF.Square,
                                 accum_out=f_cols[:, 44 + tc:45 + tc])
                e.fin(fc2.bump(), 1)
            body("scalar", f)

            def f(e, w=fc2.n):
                e.wait_ge(fc2.h, w)
                e.tensor_single_scalar(f_cols[:, 48:52], f_cols[:, 40:44],
                                       1.0 / D, OP.mult)
                e.tensor_single_scalar(f_cols[:, 52:56], f_cols[:, 44:48],
                                       1.0 / D, OP.mult).then_inc(fc2.bump(), 1)
            body("vector", f)

            def f(e, w=fc2.n):
                e.wait_ge(fc2.h, w)
                e.activation(f_cols[:, 56:60], f_cols[:, 48:52],
                             AF.Square).then_inc(fc2.bump(), 1)
            body("scalar", f)

            def f(e, w=fc2.n):
                e.wait_ge(fc2.h, w)
                e.tensor_tensor(f_cols[:, 56:60], f_cols[:, 52:56],
                                f_cols[:, 56:60], OP.subtract).then_inc(fc2.bump(), 1)
            body("vector", f)

            def f(e, w=fc2.n):
                e.wait_ge(fc2.h, w)
                e.activation(f_cols[:, 60:64], f_cols[:, 56:60], AF.Ln,
                             bias=c_eps.ap()[:, 0:1])
                e.activation(f_cols[:, 60:64], f_cols[:, 60:64], AF.Exp,
                             scale=-0.5).then_inc(fc2.bump(), 1)
            body("scalar", f)

            def f(e, w=fc2.n):
                e.wait_ge(fc2.h, w)
                e.scalar_tensor_tensor(f_cols[:, 56:60], f_cols[:, 48:52], -1.0,
                                       f_cols[:, 60:64], OP.mult,
                                       OP.mult).then_inc(fc2.bump(), 1)
            body("vector", f)
            W_FC2B = fc2.n

            if k == 0 and "colsf" in dbg:
                def f(e, w=W_FC2B):
                    e.wait_ge(fc2.h, w)
                    e.dma_start(out=dbg["colsf"], in_=f_cols.ap()
                                ).then_inc(dbgs.bump(16), 16)
                body("gpsimd", f)

            for tc in range(4):
                w_fod_rot = fod.n - 16 if tc >= 2 else 0

                def f(e, tc=tc, w=W_FC2B, wfod=w_fod_rot):
                    e.wait_ge(fc2.h, w)
                    if wfod > 0:
                        e.wait_ge(fod.h, wfod)
                    e.tensor_scalar(f_out2[:, (tc % 2) * D:(tc % 2 + 1) * D],
                                    f_bigA[:, tc * D:(tc + 1) * D],
                                    f_cols.ap()[:, 60 + tc:61 + tc],
                                    f_cols.ap()[:, 56 + tc:57 + tc],
                                    OP.mult, OP.add)
                    e.fin(fo2.bump(), 1)
                body("vector", f)
                W_FO2T = fo2.n

                def f(e, k=k, tc=tc, w=W_FO2T):
                    e.wait_ge(fo2.h, w)
                    b_, half = tc // 2, tc % 2
                    e.dma_start(out=out_ext[4 * b_ + k][half * 128:(half + 1) * 128, :],
                                in_=f_out2[:, (tc % 2) * D:(tc % 2 + 1) * D]
                                ).then_inc(fod.bump(16), 16)
                body("sync", f)

        def f(e, wfod=fod.n, wag=ag.n, wq=qwd.n, wu=ud.n,
              wxs=xsd.n, wxd=xdd.n, wc=cld.n, wzd=mzd.n):
            e.wait_ge(cld.h, wc)
            e.wait_ge(ud.h, wu)
            e.wait_ge(xsd.h, wxs)
            e.wait_ge(xdd.h, wxd)
            if wq:
                e.wait_ge(qwd.h, wq)
            if wzd:
                e.wait_ge(mzd.h, wzd)
            if wag:
                e.wait_ge(ag.h, wag)
            if wfod:
                e.wait_ge(fod.h, wfod)
            if dbgs.n:
                e.wait_ge(dbgs.h, dbgs.n)
        body("sync", f)

    return nc



def _prep_inputs(inputs, core):
    x = np.asarray(inputs["L0_attn_x"])
    xb = np.zeros((NBLK, 384, D), np.float32)
    for l in range(NBLK):
        g = 8 * l + core
        b, jj = g // 32, g % 32
        t0 = jj * TBLK
        lo = max(0, t0 - 3)
        xb[l, 3 - (t0 - lo):3 + TBLK] = x[b, lo:t0 + TBLK]
    ds = slice(core * DSL, (core + 1) * DSL)
    return {
        "x_blk": xb,
        "cw": np.ascontiguousarray(
            np.asarray(inputs["L0_attn_conv_w"])[:, 0, :].reshape(16, 128, KC)),
        "cb": np.ascontiguousarray(np.asarray(inputs["L0_attn_conv_b"]).reshape(16, 128).T),
        "xw_t": np.ascontiguousarray(
            np.asarray(inputs["L0_attn_xproj_w"]).T.reshape(16, 128, 160)),
        "dtw_t": np.ascontiguousarray(np.asarray(inputs["L0_attn_dtproj_w"])[ds].T),
        "dtb": np.ascontiguousarray(
            np.asarray(inputs["L0_attn_dtproj_b"])[ds].reshape(2, 128).T),
        "alog": np.ascontiguousarray(
            np.asarray(inputs["L0_attn_A_log"])[ds].reshape(2, 128, NST)),
        "dskip": np.ascontiguousarray(
            np.asarray(inputs["L0_attn_D_skip"])[ds].reshape(2, 128).T),
        "gup": np.ascontiguousarray(
            np.asarray(inputs["L0_ffn_gate_up_packed"])[core * 1408:(core + 1) * 1408]),
        "gsc": np.ascontiguousarray(
            np.asarray(inputs["L0_ffn_gate_up_scales"])[:, core * 1408:(core + 1) * 1408]
            .T.reshape(11, 128, 16)),
        "dnp": np.ascontiguousarray(
            np.asarray(inputs["L0_ffn_down_packed"])[core * 256:(core + 1) * 256]),
        "dsc": np.ascontiguousarray(
            np.asarray(inputs["L0_ffn_down_scales"])[:, core * 256:(core + 1) * 256]
            .T.reshape(2, 128, 44)),
        "id_f32": np.eye(128, dtype=np.float32),
        "id_b16": np.eye(128, dtype=ml_dtypes.bfloat16),
        "ones_b16": np.ones((1, 128), dtype=ml_dtypes.bfloat16),
        "onescol_f32": np.ones((128, 1), dtype=np.float32),
        "onesrow_f32": np.ones((1, 128), dtype=np.float32),
        "onescol_b16": np.ones((128, 1), dtype=ml_dtypes.bfloat16),
        "eps": np.tile(np.array([[EPS_LN, EPS_RMS]], np.float32), (128, 1)),
    }


def _sel_const():
    sel = np.zeros((32, 32 * 128), dtype=ml_dtypes.bfloat16)
    for n in range(32):
        sel[n, n * 128:(n + 1) * 128] = 1.0
    return sel


_NC_CACHE = {}


_TAPS = {
    "ucc": ([D, 2048], BF16), "umine": ([2 * 8 * 128, 2048], BF16),
    "ush0": ([D, 2048], BF16),
    "wcc": ([264 * 128, 128], BF16), "xdbl": ([160, 2048], F32),
    "mu": ([128, 4 * TSTG], BF16), "mdt": ([128, B * TSTG], F32),
    "mbc": ([32, B * TSTG], F32), "delta": ([128, 4 * TSTG], BF16),
    "z0": ([B * 2 * 128, TSTG], F32), "x1": ([128, 16 * 512], F32),
    "x1tm": ([128, 4 * D], BF16), "hh": ([128, 44 * 512], BF16),
    "y2": ([128, 4 * D], BF16), "cols": ([128, 64], F32),
    "wdn": ([128, 44 * 256], BF16), "pdn": ([128, 1024], F32),
    "r2": ([128, 16 * 512], F32), "colsf": ([128, 64], F32),
    "y2e": ([128, 512], BF16), "y2g": ([128, 512], BF16),
    "evf": ([128, 512], F32), "evb": ([128, 256], BF16),
    "y2f": ([128, 16 * 512], F32),
    "ya": ([128, 512], BF16), "yb": ([128, 512], BF16), "yc": ([128, 512], BF16),
    "yd": ([128, 512], BF16), "ye": ([128, 512], BF16), "yf": ([128, 512], BF16),
    "zz": ([16 * 128, 2048], F32),
}


def kernel(**inputs):
    import os
    ph = int(os.environ.get("KPH", "6"))
    pl = int(os.environ.get("KPRE", "5"))
    taps = tuple(t for t in os.environ.get("KTAP", "").split(",") if t)
    if taps and ("y2" in taps):
        taps = taps + ("cols",)
    tap_spec = tuple((t, _TAPS[t][0], _TAPS[t][1]) for t in taps)
    key = (ph, pl, taps)
    if key not in _NC_CACHE:
        _NC_CACHE[key] = build(debug_taps=tap_spec, phases=ph, prelvl=pl)
    nc = _NC_CACHE[key]
    in_maps = [_prep_inputs(inputs, c) for c in range(NCORES)]
    res = run_bass_kernel_spmd(nc, in_maps, core_ids=list(range(NCORES)))
    if taps:
        kernel.tap_results = [{t: res.results[c]["dbg_" + t] for t in taps}
                              for c in range(NCORES)]
    out = np.zeros((B, L, D), np.float32)
    for c in range(NCORES):
        o = res.results[c]["out"]
        for l in range(NBLK):
            g = 8 * l + c
            b, jj = g // 32, g % 32
            out[b, jj * TBLK:(jj + 1) * TBLK] = o[l]
    return out


# version 11 (final)
# speedup vs baseline: 1.0612x; 1.0612x over previous
import contextlib
import os
import numpy as np
import ml_dtypes
import concourse.bass as bass
import concourse.mybir as mybir
from concourse.bass_utils import run_bass_kernel_spmd

F32 = mybir.dt.float32
BF16 = mybir.dt.bfloat16
F16 = mybir.dt.float16
I32 = mybir.dt.int32
AF = mybir.ActivationFunctionType
OP = mybir.AluOpType
AX = mybir.AxisListType

B = 2
L = 8192
D = 2048
NST = 16
KC = 4
I_FF = 5632
NBLK = 8
TBLK = 256
NSTAGE = 4
TSTG = 2048
DSL = 256
EPS_LN = 1e-5
EPS_RMS = 1e-6
BIG = float(2.0 ** 126)
NCORES = 8


class Sem:
    __slots__ = ("h", "n")

    def __init__(self, h):
        self.h = h
        self.n = 0

    def bump(self, k=1):
        self.n += k
        return self.h


def build(debug_taps=(), phases=6, prelvl=5):
    nc = bass.Bass("TRN2", target_bir_lowering=False)
    es = contextlib.ExitStack()

    def din(name, shape, dt=F32):
        return nc.dram_tensor(name, shape, dt, kind="ExternalInput").ap()

    x_blk = din("x_blk", [NBLK, 384, D])
    cw_in = din("cw", [16, 128, KC])
    cb_in = din("cb", [128, 16])
    xw_in = din("xw_t", [16, 128, 160])
    dtw_in = din("dtw_t", [128, DSL])
    dtb_in = din("dtb", [128, 2])
    alog_in = din("alog", [2, 128, NST])
    dsk_in = din("dskip", [128, 2])
    gup_in = din("gup", [1408, 256], I32)
    gsc_in = din("gsc", [11, 128, 16], F16)
    dnp_in = din("dnp", [256, 704], I32)
    dsc_in = din("dsc", [2, 128, 44], F16)
    idf_in = din("id_f32", [128, 128])
    idb_in = din("id_b16", [128, 128], BF16)
    ones_in = din("ones_b16", [1, 128], BF16)
    onesc_in = din("onescol_f32", [128, 1])
    onesr_in = din("onesrow_f32", [1, 128])
    onescb_in = din("onescol_b16", [128, 1], BF16)
    eps_in = din("eps", [128, 2])

    out_ext = nc.dram_tensor("out", [NBLK, TBLK, D], F32, kind="ExternalOutput").ap()

    XDBL_b = [nc.dram_tensor(f"xdbl_cc{l}", [224, TBLK], F32) for l in range(NBLK)]
    XDBL_shb = [nc.dram_tensor(f"xdbl_sh{l}", [8 * 224, TBLK], F32,
                               addr_space="Shared") for l in range(NBLK)]
    U_b = [nc.dram_tensor(f"u_cc{l}", [D, TBLK], BF16) for l in range(NBLK)]
    U_shb = [nc.dram_tensor(f"u_sh{l}", [8 * D, TBLK], BF16, addr_space="Shared")
             for l in range(NBLK)]
    W_gu_cc = nc.dram_tensor("wgu_cc", [128, 176 * 128], BF16)
    W_gu_sh = nc.dram_tensor("wgu_sh", [8 * 128, 176 * 128], BF16, addr_space="Shared")
    W_dn_cc = nc.dram_tensor("wdn_cc", [128, 88 * 128], BF16)
    W_dn_sh = nc.dram_tensor("wdn_sh", [8 * 128, 88 * 128], BF16, addr_space="Shared")
    XSP = nc.dram_tensor("xspill", [NBLK, D, TBLK], F32)
    BC_stage = nc.dram_tensor("bc_stage", [2, B, 32, TSTG], F32)
    Z_ccu = [[nc.dram_tensor(f"z_cc{k}_{u}", [128, TSTG], F32) for u in range(4)]
             for k in range(NSTAGE)]
    Z_shu = [[nc.dram_tensor(f"z_sh{k}_{u}", [8 * 128, TSTG], F32,
                             addr_space="Shared") for u in range(4)]
             for k in range(NSTAGE)]

    dbg = {}
    for name, shape, dt in debug_taps:
        dbg[name] = nc.dram_tensor("dbg_" + name, list(shape), dt,
                                   kind="ExternalOutput").ap()

    with es:
        def sbuf(name, shape, dt):
            return es.enter_context(nc.sbuf_tensor(name, shape, dt))

        c_cw = sbuf("c_cw", [128, 16 * KC], F32)
        c_cb = sbuf("c_cb", [128, 16], F32)
        c_xwb = sbuf("c_xwb", [128, 16 * 160], BF16)
        c_dtwb = sbuf("c_dtwb", [128, DSL], BF16)
        c_dtb = sbuf("c_dtb", [128, 2], F32)
        c_A = sbuf("c_A", [128, 2 * NST], F32)
        c_dsk = sbuf("c_dsk", [128, 2], F32)
        c_idf = sbuf("c_idf", [128, 128], F32)
        c_idb = sbuf("c_idb", [128, 128], BF16)
        c_ones = sbuf("c_ones", [1, 128], BF16)
        c_onesc = sbuf("c_onesc", [128, 1], F32)
        c_onesr = sbuf("c_onesr", [1, 128], F32)
        c_onescb = sbuf("c_onescb", [128, 1], BF16)
        c_tmp = sbuf("c_tmp", [128, 2 * 160], F32)
        state = sbuf("state", [128, B * 2 * NST], F32)
        q_gsc = sbuf("q_gsc", [128, 11 * 16], F32)
        q_dsc = sbuf("q_dsc", [128, 2 * 44], F32)
        q_scs = sbuf("q_scs", [128, 11 * 16], F16)
        q_dss = sbuf("q_dss", [128, 2 * 44], F16)
        f_cols = sbuf("f_cols", [128, 64], F32)
        c_eps = sbuf("c_eps", [128, 2], F32)

        arena = sbuf("arena", [128, 47360], F32)
        A32 = arena.ap()
        A16 = arena.ap().bitcast(BF16)
        AI = arena.ap().bitcast(I32)

        def cv32(off, n):
            assert off % 4 == 0 and off + 4 * n <= 189440
            return A32[:, off // 4: off // 4 + n]

        def cv16(off, n):
            assert off % 4 == 0 and off + 2 * n <= 189440
            return A16[:, off // 2: off // 2 + n]

        def cvi(off, n):
            assert off % 4 == 0 and off + 4 * n <= 189440
            return AI[:, off // 4: off // 4 + n]

        p_xtm = cv32(0, 6144)
        p_xfm = cv32(24576, 6144)
        p_ub = cv16(49152, 4096)
        p_xdbl = cv32(57344, 2048)
        p_xbc = cv32(65536, 2048)
        p_csc = cv32(73728, 256)
        q_pk = cvi(0, 704)
        q_t0 = cvi(4096, 5632)
        q_bits = cvi(26624, 5632)
        q_deq = cv32(49152, 5632)
        q_wt = cv16(71680, 5632)
        m_dtb = cv16(147456, 4096)
        m_dBu = cv32(24576, 4096)
        m_Cb2 = (cv32(40960, 2048), cv32(131072, 2048))
        m_delta = cv16(49152, 8192)
        m_u = cv16(65536, 8192)
        m_du = cv16(81920, 8192)
        m_sp = cv32(98304, 2048)
        m_rel = cv32(106496, 2048)
        m_dBu2 = (m_sp, m_rel)
        m_dA = cv32(114688, 4096)
        m_h2 = (cv32(139264, 2048), cv32(155648, 2048))
        m_hC = cv16(147456, 4096)
        m_z = cv32(163840, 2048)
        f_bigA = cv32(0, 8192)
        f_x1tm = cv16(32768, 8192)
        f_x1b = cv16(49152, 8192)
        f_hh = cv16(65536, 22528)
        f_wgu = cv16(110592, 8192)
        f_wdn = cv16(126976, 11264)
        f_y2b = cv16(110592, 8192)
        f_r1sq = cv16(149504, 8192)
        f_out2 = cv32(165888, 4096)
        f_sg = cv16(182272, 512)
        f_x1t = cv32(183296, 512)
        f_sqd = cv16(185344, 2048)
        f_rows = A32[0:1, 16384:16384 + 2560]

        pMix = es.enter_context(nc.psum_tensor("pMix", [128, 2048], F32))
        pBh = es.enter_context(nc.psum_tensor("pBh", [128, 1024], F32))
        pCh = es.enter_context(nc.psum_tensor("pCh", [128, 1024], F32))

        S = {}
        for s in ["cld", "cprep", "xld", "xtr", "xev", "cv", "xpe", "xp", "ud",
                  "xdd", "xsd", "qld", "qsh", "qdq", "qtr", "qev", "qwd", "ag",
                  "mld", "mdtb", "mdp", "msp", "mdel", "mdu", "mbb", "mcb",
                  "mda", "mbu", "msc", "mhc", "myq", "mze", "mzd",
                  "fzl", "fxl", "fr1", "fmq", "fmo", "frw", "fbc", "fx1", "fxb",
                  "ftr", "fte", "fwg", "fgu", "fsl", "fhh", "fwd", "fdn", "fde",
                  "fc2", "fr2", "fo2", "fod", "dbgs", "msv", "mrw",
                  "mbb1", "mcb1", "mhcg"]:
            S[s] = Sem(es.enter_context(nc.semaphore("s_" + s)))
        (cld, cprep, xld, xtr, xev, cv, xpe, xp, ud, xdd, xsd, qld, qsh, qdq,
         qtr, qev, qwd, ag, mld, mdtb, mdp, msp, mdel, mdu, mbb, mcb, mda, mbu,
         msc, mhc, myq, mze, mzd, fzl, fxl, fr1, fmq, fmo, frw, fbc, fx1, fxb,
         ftr, fte, fwg, fgu, fsl, fhh, fwd, fdn, fde, fc2, fr2, fo2, fod) = (
            S[s] for s in ["cld", "cprep", "xld", "xtr", "xev", "cv", "xpe",
                           "xp", "ud", "xdd", "xsd", "qld", "qsh", "qdq", "qtr",
                           "qev", "qwd", "ag", "mld", "mdtb", "mdp", "msp",
                           "mdel", "mdu", "mbb", "mcb", "mda", "mbu", "msc",
                           "mhc", "myq", "mze", "mzd", "fzl", "fxl", "fr1",
                           "fmq", "fmo", "frw", "fbc", "fx1", "fxb", "ftr",
                           "fte", "fwg", "fgu", "fsl", "fhh", "fwd", "fdn",
                           "fde", "fc2", "fr2", "fo2", "fod"])

        dbgs = S["dbgs"]
        msv = S["msv"]
        mrw = S["mrw"]
        mbbP = (S["mbb"], S["mbb1"])
        mcbP = (S["mcb"], S["mcb1"])
        mhcP = (S["mhc"], S["mhcg"])
        TRIG_DONE = {}
        blk_ctx = es.enter_context(nc.Block())

        class EngTrack:

            def __init__(self, e):
                object.__setattr__(self, "_e", e)
                object.__setattr__(self, "last", None)

            def __getattr__(self, name):
                fn = getattr(self._e, name)
                if callable(fn) and name not in ("wait_ge", "nop", "then_inc"):
                    def wrap(*a, **kw):
                        r = fn(*a, **kw)
                        if hasattr(r, "then_inc"):
                            object.__setattr__(self, "last", r)
                        return r
                    return wrap
                return fn

            def fin(self, sem, k=1):
                self.last.then_inc(sem, k)

        P_ID = nc.partition_id()
        PD2 = [P_ID * 2 + 0, P_ID * 2 + 1]

        def body(engine, fn):
            getattr(blk_ctx, engine)(lambda eng: fn(EngTrack(eng)))

        def f(e):
            loads = [
                (c_cw.ap().rearrange("p (a k) -> p a k", k=KC),
                 cw_in.rearrange("a p k -> p a k")),
                (c_cb[:], cb_in),
                (c_tmp.ap().rearrange("p (a e) -> p a e", e=160),
                 xw_in.rearrange("a p e -> p a e")[:, 0:2]),
                (c_dtb[:], dtb_in),
                (c_A.ap().rearrange("p (a n) -> p a n", n=NST),
                 alog_in.rearrange("a p n -> p a n")),
                (c_dsk[:], dsk_in),
                (c_idf[:], idf_in),
                (c_idb[:], idb_in),
                (c_ones[:], ones_in),
                (c_onesc[:], onesc_in),
                (c_onesr[:], onesr_in),
                (c_onescb[:], onescb_in),
                (c_eps[:], eps_in),
                (q_scs.ap().rearrange("p (a g) -> p a g", g=16),
                 gsc_in.rearrange("a p g -> p a g")),
                (q_dss.ap().rearrange("p (a g) -> p a g", g=44),
                 dsc_in.rearrange("a p g -> p a g")),
                (m_sp[:, :DSL], dtw_in),
            ]
            for dst, src in loads:
                e.dma_start(out=dst, in_=src).then_inc(cld.bump(16), 16)
        body("sync", f)
        CLD0 = cld.n

        def f(e):
            e.wait_ge(cld.h, CLD0)
            e.activation(c_xwb[:, 0:320], c_tmp[:], AF.Copy)
            e.activation(c_dtwb[:], m_sp[:, :DSL], AF.Copy)
            e.activation(c_A[:], c_A[:], AF.Exp)
            e.activation(q_gsc[:], q_scs[:], AF.Copy)
            e.activation(q_dsc[:], q_dss[:], AF.Copy).then_inc(cprep.bump(), 1)
        body("scalar", f)

        for pair in range(1, 8):
            wtmp = cprep.n

            def f(e, pair=pair, wtmp=wtmp):
                e.wait_ge(cprep.h, wtmp)
                e.dma_start(out=c_tmp.ap().rearrange("p (a e) -> p a e", e=160),
                            in_=xw_in.rearrange("a p e -> p a e")
                            [:, 2 * pair:2 * pair + 2]).then_inc(cld.bump(16), 16)
            body("sync", f)
            wld = cld.n

            def f(e, pair=pair, wld=wld):
                e.wait_ge(cld.h, wld)
                e.activation(c_xwb[:, 320 * pair:320 * (pair + 1)], c_tmp[:],
                             AF.Copy).then_inc(cprep.bump(), 1)
            body("scalar", f)

        def f(e, w=cprep.n):
            e.wait_ge(cprep.h, w)
            e.tensor_single_scalar(c_A[:], c_A[:], -1.0, OP.mult)
            e.tensor_single_scalar(q_gsc[:], q_gsc[:], BIG, OP.mult)
            e.tensor_single_scalar(q_dsc[:], q_dsc[:], BIG, OP.mult)
            e.memset(state[:], 0.0).then_inc(cprep.bump(), 1)
        body("vector", f)
        CPREP = cprep.n

        for blk in range(NBLK if phases >= 1 else 0):
            w_prev_tr = xtr.n

            def f(e, blk=blk, w=w_prev_tr):
                if blk >= 1:
                    e.wait_ge(xtr.h, w)
                e.dma_start(out=p_xtm.rearrange("p (a d) -> p a d", d=D),
                            in_=x_blk[blk].rearrange("(a p) d -> p a d", p=128)
                            ).then_inc(xld.bump(16), 16)
            body("sync", f)
            W_XLD = xld.n
            w_prev_cv = cv.n
            w_prev_xsd = xsd.n
            ev_base = xev.n

            def f(e, blk=blk, W_XLD=W_XLD, w_cv=w_prev_cv, ev_base=ev_base):
                if prelvl < 2:
                    return
                e.wait_ge(xld.h, W_XLD)
                e.wait_ge(cld.h, CLD0)
                if blk >= 1:
                    e.wait_ge(cv.h, w_cv)
                for g in range(24):
                    if g >= 1:
                        e.wait_ge(xev.h, ev_base + g - 1)
                    for i in range(2):
                        idx = g * 2 + i
                        a, dc = idx % 3, idx // 3
                        slot = (g % 2) * 2 + i
                        e.transpose(pMix[:, slot * 512:slot * 512 + 128],
                                    p_xtm[:, a * D + dc * 128: a * D + (dc + 1) * 128],
                                    c_idf[:])
                    e.fin(xtr.bump(), 1)
            body("tensor", f)
            tr_base = xtr.n - 24

            def f(e, blk=blk, tr_base=tr_base, w_xsd=w_prev_xsd):
                if prelvl < 2:
                    return
                if blk >= 1:
                    e.wait_ge(xsd.h, w_xsd)
                for g in range(24):
                    e.wait_ge(xtr.h, tr_base + g + 1)
                    for i in range(2):
                        idx = g * 2 + i
                        a, dc = idx % 3, idx // 3
                        slot = (g % 2) * 2 + i
                        e.activation(p_xfm[:, dc * 384 + a * 128: dc * 384 + (a + 1) * 128],
                                     pMix[:, slot * 512:slot * 512 + 128], AF.Copy)
                    e.fin(xev.bump(), 1)
            body("scalar", f)
            W_XEV = xev.n
            w_prev_ud = ud.n

            def f(e, blk=blk, W_XEV=W_XEV, w_ud=w_prev_ud):
                if prelvl < 3:
                    return
                e.wait_ge(xev.h, W_XEV)
                e.wait_ge(cprep.h, CPREP)
                if blk >= 1:
                    e.wait_ge(ud.h, w_ud)
                for dc in range(16):
                    xo = dc * 384
                    e.tensor_scalar(p_csc, p_xfm[:, xo:xo + TBLK],
                                    c_cw[:, dc * KC: dc * KC + 1],
                                    c_cb[:, dc:dc + 1], OP.mult, OP.add)
                    for j in (1, 2):
                        e.scalar_tensor_tensor(p_csc, p_xfm[:, xo + j:xo + j + TBLK],
                                               c_cw[:, dc * KC + j: dc * KC + j + 1],
                                               p_csc, OP.mult, OP.add)
                    e.scalar_tensor_tensor(p_ub[:, dc * TBLK:(dc + 1) * TBLK],
                                           p_xfm[:, xo + 3:xo + 3 + TBLK],
                                           c_cw[:, dc * KC + 3: dc * KC + 4],
                                           p_csc, OP.mult, OP.add)
                e.fin(cv.bump(), 1)
            body("vector", f)
            W_CV = cv.n
            w_prev_xp = xp.n

            def f(e, blk=blk, W_CV=W_CV, w_xp=w_prev_xp):
                if prelvl < 4:
                    return
                e.wait_ge(cv.h, W_CV)
                if blk >= 1:
                    e.wait_ge(xp.h, w_xp)
                for dc in range(16):
                    e.matmul(pBh[:, :TBLK], c_xwb[:, dc * 160: dc * 160 + 128],
                             p_ub[:, dc * TBLK:(dc + 1) * TBLK],
                             start=(dc == 0), stop=(dc == 15))
                for dc in range(16):
                    e.matmul(pCh[:32, :TBLK], c_xwb[:, dc * 160 + 128: dc * 160 + 160],
                             p_ub[:, dc * TBLK:(dc + 1) * TBLK],
                             start=(dc == 0), stop=(dc == 15))
                e.fin(xpe.bump(), 1)
            body("tensor", f)
            W_XPE = xpe.n

            def f(e, blk=blk, W_XPE=W_XPE):
                if prelvl < 4:
                    return
                e.wait_ge(xpe.h, W_XPE)
                if blk >= 1:
                    e.wait_ge(xdd.h, 48 * blk)
                e.activation(p_xdbl[:, blk * TBLK:(blk + 1) * TBLK], pBh[:, :TBLK], AF.Copy)
                e.activation(c_tmp.ap().bitcast(BF16)[:, 0:TBLK], pBh[:, :TBLK],
                             AF.Copy)
                e.activation(p_xbc[:32, blk * TBLK:(blk + 1) * TBLK], pCh[:32, :TBLK],
                             AF.Copy).then_inc(xp.bump(), 1)
                e.wait_ge(xp.h, xp.n)
                e.dma_start(out=XDBL_b[blk].ap()[0:128],
                            in_=p_xdbl[:, blk * TBLK:(blk + 1) * TBLK]
                            ).then_inc(xdd.bump(16), 16)
                e.dma_start(out=XDBL_b[blk].ap()[128:160],
                            in_=p_xbc[0:32, blk * TBLK:(blk + 1) * TBLK]
                            ).then_inc(xdd.bump(16), 16)
                e.dma_start(out=XDBL_b[blk].ap()[160:224].bitcast(BF16)
                            .rearrange("a (two t) -> (a two) t", two=2),
                            in_=c_tmp.ap().bitcast(BF16)[:, 0:TBLK]
                            ).then_inc(xdd.bump(16), 16)
            body("scalar", f)

            def f(e, blk=blk, W_CV=W_CV, W_XEV=W_XEV, W_UD=ud.n + 16):
                if prelvl < 5:
                    return
                e.wait_ge(cv.h, W_CV)
                e.dma_start(out=U_b[blk].ap().rearrange("(dc p) t -> p dc t", p=128),
                            in_=p_ub.rearrange("p (dc t) -> p dc t", t=TBLK)
                            ).then_inc(ud.bump(16), 16)
                e.wait_ge(xev.h, W_XEV)
                e.dma_start(out=XSP[blk].rearrange("(dc p) t -> p dc t", p=128),
                            in_=p_xfm.rearrange("p (dc c) -> p dc c", c=384)
                            [:, :, 3:3 + TBLK]).then_inc(xsd.bump(16), 16)
                e.wait_ge(ud.h, W_UD)
                e.collective_compute("AllGather", OP.bypass,
                                     replica_groups=[list(range(8))],
                                     ins=[U_b[blk][:]], outs=[U_shb[blk][:]]
                                     ).then_inc(ag.bump(), 1)
                e.wait_ge(xdd.h, 48 * (blk + 1))
                e.collective_compute("AllGather", OP.bypass,
                                     replica_groups=[list(range(8))],
                                     ins=[XDBL_b[blk][:]], outs=[XDBL_shb[blk][:]]
                                     ).then_inc(ag.bump(), 1)
            body("gpsimd", f)

        W_XDD = xdd.n

        def emit_dequant(words, hgrp, sc_tile, sc_off, src_ap, dst_rows):
            nfree = words * 8
            w_qsh = qsh.n

            first = DEQ_GUARDS.pop("first", False)

            def f(e, w=w_qsh, first=first):
                if first:
                    e.wait_ge(xtr.h, DEQ_GUARDS["xtr"])
                e.wait_ge(qsh.h, w)
                e.dma_start(out=q_pk[:, :words], in_=src_ap).then_inc(qld.bump(16), 16)
            body("sync", f)
            w_qld = qld.n
            w_qtr_prev = qtr.n

            def f(e, w_qld=w_qld, words=words, nfree=nfree, hgrp=hgrp,
                  sc_off=sc_off, sc_tile=sc_tile, w_qtr=w_qtr_prev, first=first):
                e.wait_ge(qld.h, w_qld)
                e.wait_ge(cprep.h, CPREP)
                if first:
                    e.wait_ge(xdd.h, DEQ_GUARDS["xdd"])
                    e.wait_ge(ud.h, DEQ_GUARDS["ud"])
                    e.wait_ge(xsd.h, DEQ_GUARDS["xsd"])
                for j in range(8):
                    e.tensor_single_scalar(q_t0[:, j:nfree:8], q_pk[:, :words],
                                           4 * j, OP.logical_shift_right)
                e.fin(qsh.bump(), 1)
                e.wait_ge(qtr.h, w_qtr)
                e.tensor_scalar(q_bits[:, :nfree], q_t0[:, :nfree], 7, 22,
                                OP.bitwise_and, OP.logical_shift_left)
                e.tensor_scalar(q_t0[:, :nfree], q_t0[:, :nfree], 8, 28,
                                OP.bitwise_and, OP.logical_shift_left)
                e.tensor_tensor(q_bits[:, :nfree], q_bits[:, :nfree],
                                q_t0[:, :nfree], OP.bitwise_or)
                bf = q_bits.bitcast(F32)
                for g in range(hgrp):
                    e.tensor_scalar(q_deq[:, g * 128:(g + 1) * 128],
                                    bf[:, g * 128:(g + 1) * 128],
                                    sc_tile[:, sc_off + g: sc_off + g + 1], None, OP.mult)
                e.fin(qdq.bump(), 1)
            body("vector", f)
            w_qdq = qdq.n
            ngr = (hgrp + 1) // 2
            trb, evb = qtr.n, qev.n
            w_qwd_prev = qwd.n

            def f(e, w_qdq=w_qdq, hgrp=hgrp, ngr=ngr, trb=trb, evb=evb,
                  w_qwd=w_qwd_prev):
                e.wait_ge(qdq.h, w_qdq)
                e.wait_ge(qwd.h, w_qwd)
                for g in range(ngr):
                    if g >= 1:
                        e.wait_ge(qev.h, evb + g - 1)
                    for i in range(min(2, hgrp - g * 2)):
                        gg = g * 2 + i
                        slot = (g % 2) * 2 + i
                        e.transpose(pMix[:, slot * 512:slot * 512 + 128],
                                    q_deq[:, gg * 128:(gg + 1) * 128], c_idf[:])
                    e.fin(qtr.bump(), 1)
            body("tensor", f)

            def f(e, hgrp=hgrp, ngr=ngr, trb=trb):
                for g in range(ngr):
                    e.wait_ge(qtr.h, trb + g + 1)
                    for i in range(min(2, hgrp - g * 2)):
                        gg = g * 2 + i
                        slot = (g % 2) * 2 + i
                        e.activation(q_wt[:, gg * 128:(gg + 1) * 128],
                                     pMix[:, slot * 512:slot * 512 + 128], AF.Copy)
                    e.fin(qev.bump(), 1)
            body("scalar", f)
            w_qev = qev.n

            def f(e, w_qev=w_qev, dst_ap=dst_rows, nfree=nfree):
                e.wait_ge(qev.h, w_qev)
                e.dma_start(out=dst_ap,
                            in_=q_wt[:, 0:nfree].rearrange("p (ic c) -> p ic c",
                                                           c=128)
                            ).then_inc(qwd.bump(16), 16)
            body("gpsimd", f)

        DEQ_GUARDS = dict(first=True, xtr=xtr.n, xdd=xdd.n, ud=ud.n, xsd=xsd.n,
                          xpe=xpe.n)
        for oc in range(11 if phases >= 2 else 0):
            emit_dequant(256, 16, q_gsc, oc * 16,
                         gup_in.rearrange("(a p) w -> a p w", p=128)[oc],
                         W_gu_cc.ap().rearrange("p (t c) -> p t c", c=128)
                         [:, oc * 16:(oc + 1) * 16])
        W_QWD_GU = qwd.n
        for hc2 in range(2 if phases >= 2 else 0):
            emit_dequant(704, 44, q_dsc, hc2 * 44,
                         dnp_in.rearrange("(a p) w -> a p w", p=128)[hc2],
                         W_dn_cc.ap().rearrange("p (ic two c) -> p ic two c",
                                                two=2, c=128)[:, :, hc2])
        W_QEV_ALL = qev.n
        W_QWD_ALL = qwd.n

        AG_ULAST = ag.n

        def f(e, wg=W_QWD_GU, ww=W_QWD_ALL):
            if phases < 3:
                return
            e.wait_ge(qwd.h, wg)
            e.collective_compute("AllGather", OP.bypass, replica_groups=[list(range(8))],
                                 ins=[W_gu_cc[:]], outs=[W_gu_sh[:]]).then_inc(ag.bump(), 1)
            e.wait_ge(qwd.h, ww)
            e.collective_compute("AllGather", OP.bypass, replica_groups=[list(range(8))],
                                 ins=[W_dn_cc[:]], outs=[W_dn_sh[:]]).then_inc(ag.bump(), 1)
        body("gpsimd", f)
        AG_WGU = AG_ULAST + 1
        AG_WDN = AG_ULAST + 2
        AG_STAGE = [2 * (4 + k) + 2 for k in range(NSTAGE)]

        XDVB = [XDBL_shb[l].ap().rearrange("(r e) t -> r e t", r=8)
                for l in range(NBLK)]

        NK = 0 if phases < 4 else (1 if phases == 4 else NSTAGE)
        for k in range(NK):
            w_mze_prev = mze.n
            w_mda_prev = mda.n
            w_fdn_prev = fdn.n
            w_fde_prev = fde.n
            w_fo2_prev = fo2.n
            w_fod_prev0 = fod.n
            w_trig_prev = TRIG_DONE.get(k - 2)

            def f(e, k=k, w2=w_mze_prev, w3=w_fdn_prev, wt=w_trig_prev):
                e.wait_ge(ag.h, AG_STAGE[k])
                if k == 0:
                    e.wait_ge(qwd.h, W_QWD_ALL)
                else:
                    e.wait_ge(mze.h, w2)
                    if w3 > 0:
                        e.wait_ge(fdn.h, w3)
                for b in range(B):
                    x16 = XDBL_shb[b * 4 + k].ap().bitcast(BF16).rearrange(
                        "(r a) tt -> r a tt", r=8)[:, 160:224, :].rearrange(
                        "r a (two t) -> r (a two) t", two=2)
                    e.dma_start(out=m_dtb[:, b * TSTG:(b + 1) * TSTG]
                                .rearrange("p (j t) -> p j t", t=TBLK),
                                in_=x16.transpose([1, 0, 2])
                                ).then_inc(mld.bump(16), 16)
                    for dblk in range(2):
                        usrc = U_shb[b * 4 + k].ap().rearrange(
                            "(r dc p) t -> dc r p t", r=8, p=128)
                        e.dma_start(out=m_u[:, (b * 2 + dblk) * TSTG:
                                                 (b * 2 + dblk + 1) * TSTG]
                                    .rearrange("p (j t) -> p j t", t=TBLK),
                                    in_=usrc[bass.ts(PD2[dblk], 1)][0]
                                    .transpose([1, 0, 2])).then_inc(mld.bump(16), 16)
                if wt is not None:
                    e.wait_ge(mbbP[0].h, wt[0])
                    e.wait_ge(mbbP[1].h, wt[1])
                    e.wait_ge(mcbP[0].h, wt[2])
                    e.wait_ge(mcbP[1].h, wt[3])
                for b in range(B):
                    e.dma_start(out=BC_stage.ap()[k % 2, b].rearrange(
                                    "e (j t) -> e j t", t=TBLK),
                                in_=XDVB[b * 4 + k][:, 128:160, :]
                                .transpose([1, 0, 2])).then_inc(mrw.bump(16), 16)
            body("gpsimd", f)
            W_MLD = mld.n
            W_ROWS = mrw.n

            if k == 0 and "mu" in dbg:
                def f(e, w=W_MLD):
                    e.wait_ge(mld.h, w)
                    e.dma_start(out=dbg["mu"], in_=m_u).then_inc(dbgs.bump(16), 16)
                body("scalar", f)

            for b in range(B):
                for dblk in range(2):
                    bd = b * 2 + dblk
                    w_msp_prev = msp.n

                    w_mze_dt = mze.n

                    def f(e, b=b, dblk=dblk, k=k, w1=W_MLD, w2=w_msp_prev,
                          wq=W_QEV_ALL, w3=w_mze_dt, w4=w_fde_prev):
                        e.wait_ge(mld.h, w1)
                        e.wait_ge(qev.h, wq)
                        if w4 > 0:
                            e.wait_ge(fde.h, w4)
                        if w3 > 0:
                            e.wait_ge(mze.h, w3)
                        if not (k == 0 and b == 0 and dblk == 0):
                            e.wait_ge(msp.h, w2)
                        for q in range(4):
                            e.matmul(pMix[:, q * 512:(q + 1) * 512],
                                     c_dtwb[:, dblk * 128:(dblk + 1) * 128],
                                     m_dtb[:, b * TSTG + q * 512: b * TSTG + (q + 1) * 512],
                                     start=True, stop=True)
                        e.fin(mdp.bump(), 1)
                    body("tensor", f)
                    W_MDP = mdp.n

                    w_mdu_sp = mdu.n

                    def f(e, dblk=dblk, w=W_MDP, wdu=w_mdu_sp):
                        e.wait_ge(mdp.h, w)
                        if wdu > 0:
                            e.wait_ge(mdu.h, wdu)
                        for q in range(4):
                            sl = slice(q * 512, (q + 1) * 512)
                            psl = pMix[:, q * 512:(q + 1) * 512]
                            bias = c_dtb[:, dblk:dblk + 1]
                            e.activation(m_sp[:, sl], psl, AF.Abs, bias=bias)
                            e.activation(m_sp[:, sl], m_sp[:, sl], AF.Exp, scale=-1.0)
                            e.activation(m_sp[:, sl], m_sp[:, sl], AF.Ln, bias=1.0)
                            e.activation(m_rel[:, sl], psl, AF.Relu, bias=bias)
                        e.fin(msp.bump(), 1)
                    body("scalar", f)
                    W_MSP = msp.n

                    def f(e, bd=bd, k=k, w=W_MSP, wda=w_mda_prev,
                          wld=W_MLD):
                        e.wait_ge(msp.h, w)
                        e.wait_ge(mld.h, wld)
                        if k >= 1:
                            e.wait_ge(mda.h, wda)
                        e.tensor_tensor(m_delta[:, bd * TSTG:(bd + 1) * TSTG],
                                        m_sp, m_rel, OP.add)
                        e.tensor_tensor(m_du[:, bd * TSTG:(bd + 1) * TSTG],
                                        m_delta[:, bd * TSTG:(bd + 1) * TSTG],
                                        m_u[:, bd * TSTG:(bd + 1) * TSTG],
                                        OP.mult).then_inc(mdu.bump(), 1)
                    body("vector", f)
            W_MDU = mdu.n

            if k == 0 and "delta" in dbg:
                def f(e, w=W_MDU):
                    e.wait_ge(mdu.h, w)
                    e.dma_start(out=dbg["delta"], in_=m_delta).then_inc(dbgs.bump(16), 16)
                body("scalar", f)

            for b in range(B):
                for dblk in range(2):
                    bd = b * 2 + dblk
                    dsl = slice(bd * TSTG, (bd + 1) * TSTG)
                    base_da = mda.n
                    base_sc = msc.n
                    base_bb = (mbbP[0].n, mbbP[1].n)
                    base_cb = (mcbP[0].n, mcbP[1].n)
                    base_hc = (mhcP[0].n, mhcP[1].n)
                    base_yq = myq.n
                    base_sv = msv.n
                    base_bu = mbu.n
                    w_mze_u = mze.n
                    w_mzd_u = mzd.n

                    def f(e, b=b, k=k, w=W_MDU, W_ROWS=W_ROWS,
                          base_sc=base_sc, base_hc=base_hc, wf=w_fo2_prev):
                        if wf > 0:
                            e.wait_ge(fo2.h, wf)
                        e.wait_ge(mrw.h, W_ROWS)
                        e.wait_ge(mdu.h, w)
                        for n in range(NST):
                            wv = base_sc + max(0, n - 1)
                            if wv > 0:
                                e.wait_ge(msc.h, wv)
                            e.dma_start(
                                out=m_dBu[:, (n % 2) * TSTG:(n % 2 + 1) * TSTG],
                                in_=BC_stage.ap()[k % 2, b, n:n + 1, :]
                                .partition_broadcast(128)[:, 0]
                                ).then_inc(mbbP[n % 2].bump(16), 16)
                            wv = base_hc[n % 2] + (n // 2 if n >= 2 else 0)
                            if wv > 0:
                                e.wait_ge(mhcP[n % 2].h, wv)
                            e.dma_start(
                                out=m_Cb2[n % 2],
                                in_=BC_stage.ap()[k % 2, b, 16 + n:17 + n, :]
                                .partition_broadcast(128)[:, 0]
                                ).then_inc(mcbP[n % 2].bump(16), 16)
                    body("sync", f)

                    def f(e, dblk=dblk, dsl=dsl, bd=bd, w=W_MDU,
                          base_sc=base_sc, base_da=base_da):
                        e.wait_ge(mdu.h, w)
                        for n in range(NST):
                            if base_sc + n - 1 > 0:
                                e.wait_ge(msc.h, base_sc + n - 1)
                            e.activation(m_dA[:, (n % 2) * TSTG:(n % 2 + 1) * TSTG],
                                         m_delta[:, dsl], AF.Exp,
                                         scale=c_A[:, dblk * NST + n: dblk * NST + n + 1]
                                         ).then_inc(mda.bump(), 1)
                            if n >= 1:
                                e.wait_ge(msc.h, base_sc + n)
                                e.activation(state[:, bd * NST + n - 1: bd * NST + n],
                                             m_h2[(n - 1) % 2][:, TSTG - 1:TSTG],
                                             AF.Copy).then_inc(msv.bump(), 1)
                        e.wait_ge(msc.h, base_sc + NST)
                        e.activation(state[:, bd * NST + NST - 1: bd * NST + NST],
                                     m_h2[(NST - 1) % 2][:, TSTG - 1:TSTG],
                                     AF.Copy).then_inc(msv.bump(), 1)
                    body("scalar", f)

                    KGPS = int(os.environ.get("KGPS", "1"))
                    if KGPS:
                        def f(e, dsl=dsl, w=W_MDU, base_bb=base_bb):
                            e.wait_ge(mdu.h, w)
                            for n in range(NST):
                                e.wait_ge(mbbP[n % 2].h,
                                          base_bb[n % 2] + 16 * (n // 2 + 1))
                                e.tensor_tensor(m_dBu2[n % 2],
                                                m_du[:, dsl],
                                                m_dBu[:, (n % 2) * TSTG:(n % 2 + 1) * TSTG],
                                                OP.mult).then_inc(mbu.bump(), 1)
                        body("gpsimd", f)

                    def f(e, bd=bd, dblk=dblk, dsl=dsl, base_bu=base_bu,
                          base_da=base_da, base_sv=base_sv, base_hc=base_hc,
                          base_yq=base_yq, base_cb=base_cb, base_bb=base_bb,
                          w_mzd_u=w_mzd_u, KGPS=KGPS,
                          w_fod0=(w_fod_prev0 if bd == 0 else 0)):
                        for n in range(NST):
                            par = n % 2
                            if KGPS:
                                e.wait_ge(mbu.h, base_bu + n + 1)
                            else:
                                e.wait_ge(mbbP[par].h,
                                          base_bb[par] + 16 * (n // 2 + 1))
                                e.tensor_tensor(m_dBu2[par], m_du[:, dsl],
                                                m_dBu[:, par * TSTG:(par + 1) * TSTG],
                                                OP.mult).then_inc(mbu.bump(), 1)
                            e.wait_ge(mda.h, base_da + n + 1)
                            wv = base_sv + max(0, n - 1)
                            if wv > 0:
                                e.wait_ge(msv.h, wv)
                            wv = base_hc[par] + (n // 2 if n >= 2 else 0)
                            if wv > 0:
                                e.wait_ge(mhcP[par].h, wv)
                            scol = state[:, bd * NST + n: bd * NST + n + 1]
                            e.tensor_tensor_scan(m_h2[par],
                                                 m_dA[:, par * TSTG:(par + 1) * TSTG],
                                                 m_dBu2[par],
                                                 scol, OP.mult,
                                                 OP.add).then_inc(msc.bump(), 1)
                            e.wait_ge(mcbP[par].h,
                                      base_cb[par] + 16 * (n // 2 + 1))
                            if base_yq + n - 1 > 0:
                                e.wait_ge(myq.h, base_yq + n - 1)
                            e.tensor_tensor(m_hC[:, par * TSTG:(par + 1) * TSTG],
                                            m_h2[par], m_Cb2[par],
                                            OP.mult).then_inc(mhcP[par].bump(), 1)
                        e.wait_ge(myq.h, base_yq + NST)
                        if w_fod0 > 0:
                            e.wait_ge(fod.h, w_fod0)
                        if w_mzd_u > 0:
                            e.wait_ge(mzd.h, w_mzd_u)
                        e.tensor_scalar(m_z, m_u[:, dsl],
                                        c_dsk[:, dblk:dblk + 1], None, OP.mult)
                        e.tensor_tensor(m_z, pMix[:], m_z,
                                        OP.add).then_inc(mze.bump(), 1)
                    body("vector", f)

                    def f(e, bd=bd, k=k, wze=w_mze_u, base_hc=base_hc):
                        if not (k == 0 and bd == 0):
                            e.wait_ge(mze.h, wze)
                        for n in range(NST):
                            e.wait_ge(mhcP[n % 2].h, base_hc[n % 2] + n // 2 + 1)
                            hcb = (n % 2) * TSTG
                            for q in range(4):
                                e.matmul(pMix[:, q * 512:(q + 1) * 512], c_idb[:],
                                         m_hC[:, hcb + q * 512: hcb + (q + 1) * 512],
                                         start=(n == 0), stop=(n == NST - 1))
                            e.fin(myq.bump(), 1)
                    body("tensor", f)
                    W_MZE = mze.n

                    def f(e, k=k, bd=bd, w=W_MZE, wzd=mzd.n + 16):
                        e.wait_ge(mze.h, w)
                        e.dma_start(out=Z_ccu[k][bd][:],
                                    in_=m_z).then_inc(mzd.bump(16), 16)
                        e.wait_ge(mzd.h, wzd)
                        e.collective_compute("AllGather", OP.bypass,
                                             replica_groups=[list(range(8))],
                                             ins=[Z_ccu[k][bd][:]],
                                             outs=[Z_shu[k][bd][:]]
                                             ).then_inc(ag.bump(), 1)
                    body("gpsimd", f)

            TRIG_DONE[k] = (mbbP[0].n, mbbP[1].n, mcbP[0].n, mcbP[1].n)
            W_MZD = mzd.n
            AG_ZK = ag.n

            if "zz" in dbg:
                def f(e, k=k, w=W_MZD):
                    e.wait_ge(mzd.h, w)
                    for u in range(4):
                        e.dma_start(out=dbg["zz"].rearrange(
                                        "(kk u p) t -> kk u p t", kk=4, u=4)[k, u],
                                    in_=Z_ccu[k][u][:]).then_inc(dbgs.bump(16), 16)
                body("scalar", f)

            if phases < 6:
                continue
            w_fod_prev = fod.n

            def f(e, k=k, wagz=AG_ZK, wfod=w_fod_prev):
                e.wait_ge(ag.h, wagz)
                if wfod > 0:
                    e.wait_ge(fod.h, wfod)
                for b in range(B):
                    for db in range(2):
                        zsrc = Z_shu[k][b * 2 + db].ap().rearrange(
                            "(r p) t -> r p t", p=128)
                        e.dma_start(
                            out=f_bigA.rearrange("p (dc b2 t) -> p dc b2 t",
                                                 b2=B, t=TBLK)[:, db::2, b],
                            in_=zsrc[:, :, bass.ts(P_ID, TBLK)].transpose([1, 0, 2])
                            ).then_inc(fzl.bump(16), 16)
                if wfod > 0:
                    e.wait_ge(fod.h, wfod)
                e.dma_start(out=f_out2.rearrange("p (dc t) -> p dc t", t=TBLK),
                            in_=XSP[4 * 0 + k].rearrange("(dc p) t -> p dc t", p=128)
                            ).then_inc(fxl.bump(16), 16)
            body("sync", f)
            W_FZL, W_FXL0 = fzl.n, fxl.n

            def f(e, w1=W_FZL, w2=W_FXL0):
                e.wait_ge(fzl.h, w1)
                e.wait_ge(fxl.h, w2)
                e.tensor_tensor(f_bigA.rearrange("p (dc b2 t) -> p dc b2 t",
                                                 b2=B, t=TBLK)[:, :, 0],
                                f_bigA.rearrange("p (dc b2 t) -> p dc b2 t",
                                                 b2=B, t=TBLK)[:, :, 0],
                                f_out2.rearrange("p (dc t) -> p dc t", t=TBLK),
                                OP.add).then_inc(fr1.bump(), 1)
            body("vector", f)
            w_fr1_b0 = fr1.n

            def f(e, k=k, w=w_fr1_b0):
                e.wait_ge(fr1.h, w)
                e.dma_start(out=f_out2.rearrange("p (dc t) -> p dc t", t=TBLK),
                            in_=XSP[4 * 1 + k].rearrange("(dc p) t -> p dc t", p=128)
                            ).then_inc(fxl.bump(16), 16)
            body("sync", f)
            W_FXL1 = fxl.n

            def f(e, w2=W_FXL1):
                e.wait_ge(fxl.h, w2)
                e.tensor_tensor(f_bigA.rearrange("p (dc b2 t) -> p dc b2 t",
                                                 b2=B, t=TBLK)[:, :, 1],
                                f_bigA.rearrange("p (dc b2 t) -> p dc b2 t",
                                                 b2=B, t=TBLK)[:, :, 1],
                                f_out2.rearrange("p (dc t) -> p dc t", t=TBLK),
                                OP.add).then_inc(fr1.bump(), 1)
            body("vector", f)
            W_FR1 = fr1.n

            def f(e, w=W_FR1):
                e.wait_ge(fr1.h, w)
                e.activation(f_x1b, f_bigA, AF.Copy)
                e.activation(f_r1sq, f_bigA, AF.Square).then_inc(fmq.bump(), 1)
            body("scalar", f)
            W_FMQ = fmq.n

            def f(e, w=W_FMQ):
                e.wait_ge(fmq.h, w)
                for dc in range(16):
                    e.matmul(pBh[0:1, 0:512], c_onescb[:],
                             f_x1b[:, dc * 512:(dc + 1) * 512],
                             start=(dc == 0), stop=(dc == 15))
                for dc in range(16):
                    e.matmul(pBh[0:1, 512:1024], c_onescb[:],
                             f_r1sq[:, dc * 512:(dc + 1) * 512],
                             start=(dc == 0), stop=(dc == 15))
                e.fin(fmo.bump(), 1)
            body("tensor", f)
            W_FMO = fmo.n

            if k == 0 and "ya" in dbg:
                def f(e, w=W_FMO):
                    e.wait_ge(fmo.h, w)
                    e.dma_start(out=dbg["ya"], in_=f_r1sq[:, 0:512]).then_inc(dbgs.bump(16), 16)
                body("gpsimd", f)

            def f(e, w=W_FMO):
                e.wait_ge(fmo.h, w)
                e.activation(f_rows[:, 0:512], pBh[0:1, 0:512], AF.Copy, scale=1.0 / D)
                e.activation(f_rows[:, 512:1024], pBh[0:1, 512:1024], AF.Copy,
                             scale=1.0 / D)
                e.activation(f_rows[:, 1024:1536], f_rows[:, 0:512],
                             AF.Square).then_inc(frw.bump(), 1)
            body("scalar", f)

            def f(e, w=frw.n):
                e.wait_ge(frw.h, w)
                e.tensor_tensor(f_rows[:, 1024:1536], f_rows[:, 512:1024],
                                f_rows[:, 1024:1536], OP.subtract).then_inc(frw.bump(), 1)
            body("vector", f)

            def f(e, w=frw.n):
                e.wait_ge(frw.h, w)
                e.activation(f_rows[:, 1536:2048], f_rows[:, 1024:1536], AF.Ln,
                             bias=c_eps.ap()[0:1, 0:1])
                e.activation(f_rows[:, 1536:2048], f_rows[:, 1536:2048], AF.Exp,
                             scale=-0.5).then_inc(frw.bump(), 1)
            body("scalar", f)

            def f(e, w=frw.n):
                e.wait_ge(frw.h, w)
                e.scalar_tensor_tensor(f_rows[:, 2048:2560], f_rows[:, 0:512], -1.0,
                                       f_rows[:, 1536:2048], OP.mult,
                                       OP.mult).then_inc(frw.bump(), 1)
            body("vector", f)
            W_FRW = frw.n

            def f(e, w=W_FRW):
                e.wait_ge(frw.h, w)
                e.matmul(pMix[:, 1024:1536], c_onesr[:], f_rows[:, 1536:2048],
                         start=True, stop=True)
                e.matmul(pMix[:, 1536:2048], c_onesr[:], f_rows[:, 2048:2560],
                         start=True, stop=True)
                e.fin(fbc.bump(), 1)
            body("tensor", f)
            W_FBC = fbc.n

            def f(e, w=W_FBC):
                e.wait_ge(fbc.h, w)
                for dc in range(16):
                    sl = slice(dc * 512, (dc + 1) * 512)
                    e.tensor_tensor(f_x1t, f_bigA[:, sl],
                                    pMix[:, 1024:1536], OP.mult)
                    e.tensor_tensor(f_bigA[:, sl], f_x1t,
                                    pMix[:, 1536:2048], OP.add)
                e.fin(fx1.bump(), 1)
            body("vector", f)
            W_FX1 = fx1.n

            def f(e, w=W_FX1):
                e.wait_ge(fx1.h, w)
                e.activation(f_x1b, f_bigA, AF.Copy).then_inc(fxb.bump(), 1)
            body("scalar", f)
            W_FXB = fxb.n

            if k == 0 and "x1" in dbg:
                def f(e, w=W_FX1):
                    e.wait_ge(fx1.h, w)
                    e.dma_start(out=dbg["x1"], in_=f_bigA).then_inc(dbgs.bump(16), 16)
                body("gpsimd", f)
            if k == 0 and "yb" in dbg:
                def f(e, w=W_FXB):
                    e.wait_ge(fxb.h, w)
                    e.dma_start(out=dbg["yb"], in_=f_r1sq[:, 0:512]).then_inc(dbgs.bump(16), 16)
                body("gpsimd", f)
            trb, teb = ftr.n, fte.n

            def f(e, w=W_FX1, trb=trb, teb=teb):
                e.wait_ge(fx1.h, w)
                for g in range(32):
                    if g >= 1:
                        e.wait_ge(fte.h, teb + g - 1)
                    for i in range(2):
                        idx = g * 2 + i
                        tc, dc = idx // 16, idx % 16
                        slot = (g % 2) * 2 + i
                        e.transpose(pMix[:, slot * 512:slot * 512 + 128],
                                    f_bigA[:, dc * 512 + tc * 128:
                                           dc * 512 + (tc + 1) * 128],
                                    c_idf[:])
                    e.fin(ftr.bump(), 1)
            body("tensor", f)

            def f(e, trb=trb):
                for g in range(32):
                    e.wait_ge(ftr.h, trb + g + 1)
                    for i in range(2):
                        idx = g * 2 + i
                        tc, dc = idx // 16, idx % 16
                        slot = (g % 2) * 2 + i
                        e.activation(f_x1tm[:, tc * D + dc * 128:
                                            tc * D + (dc + 1) * 128],
                                     pMix[:, slot * 512:slot * 512 + 128], AF.Copy)
                    e.fin(fte.bump(), 1)
            body("scalar", f)
            W_FTE = fte.n

            if k == 0 and "yc" in dbg:
                def f(e, w=W_FTE):
                    e.wait_ge(fte.h, w)
                    e.dma_start(out=dbg["yc"], in_=f_r1sq[:, 0:512]).then_inc(dbgs.bump(16), 16)
                body("gpsimd", f)
            if k == 0 and "x1tm" in dbg:
                def f(e, w=W_FTE):
                    e.wait_ge(fte.h, w)
                    e.dma_start(out=dbg["x1tm"], in_=f_x1tm).then_inc(dbgs.bump(16), 16)
                body("gpsimd", f)
            FGU_AT, FSL_AT, FHH_AT = [], [], []
            for ocp in range(44):
                pg = pBh if ocp % 2 == 0 else pCh
                wbuf = (ocp % 2) * 2 * 16 * 128
                w_fgu_m2 = FGU_AT[ocp - 2] if ocp >= 2 else (fgu.n if k > 0 else 0)
                w_fr2_prev = fr2.n if (ocp < 2 and k > 0) else 0

                def f(e, ocp=ocp, wbuf=wbuf, w_fgu=w_fgu_m2, wr2=w_fr2_prev):
                    e.wait_ge(ag.h, AG_WGU)
                    if wr2 > 0:
                        e.wait_ge(fr2.h, wr2)
                    if w_fgu > 0:
                        e.wait_ge(fgu.h, w_fgu)
                    for half, oc in ((0, ocp), (1, 44 + ocp)):
                        r, ocl = oc // 11, oc % 11
                        e.dma_start(
                            out=f_wgu[:, wbuf + half * 16 * 128:
                                           wbuf + (half + 1) * 16 * 128],
                            in_=W_gu_sh.ap()[128 * r:128 * (r + 1),
                                             ocl * 2048:(ocl + 1) * 2048]
                            ).then_inc(fwg.bump(16), 16)
                body("sync", f)
                W_FWG = fwg.n
                w_fhh_m2 = FHH_AT[ocp - 2] if ocp >= 2 else (fhh.n if k > 0 else 0)

                def f(e, ocp=ocp, pg=pg, wbuf=wbuf, w=W_FWG, wxb=W_FXB,
                      w_fhh=w_fhh_m2):
                    e.wait_ge(fwg.h, w)
                    e.wait_ge(fxb.h, wxb)
                    if w_fhh > 0:
                        e.wait_ge(fhh.h, w_fhh)
                    for half in range(2):
                        for hc in range(16):
                            e.matmul(pg[:, half * 512:(half + 1) * 512],
                                     f_wgu[:, wbuf + half * 16 * 128 + hc * 128:
                                           wbuf + half * 16 * 128 + (hc + 1) * 128],
                                     f_x1b[:, hc * 512:(hc + 1) * 512],
                                     start=(hc == 0), stop=(hc == 15))
                    e.fin(fgu.bump(), 1)
                body("tensor", f)
                FGU_AT.append(fgu.n)
                w_fhh_m1 = FHH_AT[ocp - 1] if ocp >= 1 else (fhh.n if k > 0 else 0)

                def f(e, pg=pg, w=fgu.n, w_fhh=w_fhh_m1):
                    e.wait_ge(fgu.h, w)
                    if w_fhh > 0:
                        e.wait_ge(fhh.h, w_fhh)
                    e.activation(f_sg, pg[:, 0:512], AF.Silu).then_inc(fsl.bump(), 1)
                body("scalar", f)
                FSL_AT.append(fsl.n)

                def f(e, pg=pg, ocp=ocp, w=fsl.n):
                    e.wait_ge(fsl.h, w)
                    e.tensor_tensor(f_hh[:, ocp * 512:(ocp + 1) * 512], f_sg,
                                    pg[:, 512:1024], OP.mult).then_inc(fhh.bump(), 1)
                body("vector", f)
                FHH_AT.append(fhh.n)
            W_FHH = fhh.n

            if k == 0 and "yd" in dbg:
                def f(e, w=fhh.n):
                    e.wait_ge(fhh.h, w)
                    e.dma_start(out=dbg["yd"], in_=f_r1sq[:, 0:512]).then_inc(dbgs.bump(16), 16)
                body("gpsimd", f)
            if k == 0 and "hh" in dbg:
                def f(e, w=W_FHH):
                    e.wait_ge(fhh.h, w)
                    e.dma_start(out=dbg["hh"], in_=f_hh).then_inc(dbgs.bump(16), 16)
                body("gpsimd", f)
            for hcc in range(8):
                w_fde_prev = fde.n

                def f(e, hcc=hcc, k=k, w=w_fde_prev):
                    e.wait_ge(ag.h, AG_WDN)
                    if w > 0:
                        e.wait_ge(fde.h, w)
                    e.dma_start(
                        out=f_wdn[:, 0:88 * 128],
                        in_=W_dn_sh.ap()[128 * hcc:128 * (hcc + 1), :]
                        ).then_inc(fwd.bump(16), 16)
                body("sync", f)
                W_FWD = fwd.n

                def f(e, hcc=hcc, w=W_FWD, whh=W_FHH, wde=w_fde_prev, k=k,
                      wte=W_FTE):
                    e.wait_ge(fwd.h, w)
                    e.wait_ge(fhh.h, whh)
                    e.wait_ge(fte.h, wte)
                    if wde > 0:
                        e.wait_ge(fde.h, wde)
                    for tc in range(4):
                        for ic in range(44):
                            e.matmul(pMix[:, tc * 512:tc * 512 + 256],
                                     f_hh[:, ic * 512 + tc * 128:
                                          ic * 512 + (tc + 1) * 128],
                                     f_wdn[:, ic * 256:(ic + 1) * 256],
                                     start=(ic == 0), stop=(ic == 43))
                    e.fin(fdn.bump(), 1)
                body("tensor", f)
                W_FDN = fdn.n

                def f(e, hcc=hcc, w=W_FDN):
                    e.wait_ge(fdn.h, w)
                    for tc in range(4):
                        e.tensor_copy(f_bigA[:, tc * D + hcc * 256:
                                             tc * D + (hcc + 1) * 256],
                                      pMix[:, tc * 512:tc * 512 + 256])
                    e.fin(fde.bump(), 1)
                body("vector", f)
            W_FDE = fde.n

            if k == 0 and "y2f" in dbg:
                def f(e, w=W_FDE):
                    e.wait_ge(fde.h, w)
                    e.dma_start(out=dbg["y2f"], in_=f_bigA).then_inc(dbgs.bump(16), 16)
                body("gpsimd", f)

            def f(e, w=W_FDE):
                e.wait_ge(fde.h, w)
                for tc in range(4):
                    e.activation(f_sqd, f_bigA[:, tc * D:(tc + 1) * D], AF.Square,
                                 accum_out=f_cols[:, 32 + tc:33 + tc])
                e.activation(f_cols[:, 36:40], f_cols[:, 32:36], AF.Ln,
                             scale=1.0 / D, bias=c_eps.ap()[:, 1:2])
                e.activation(f_cols[:, 36:40], f_cols[:, 36:40], AF.Exp,
                             scale=-0.5).then_inc(fc2.bump(), 1)
            body("scalar", f)
            W_FC2A = fc2.n

            def f(e, w=W_FC2A, wte=W_FTE):
                e.wait_ge(fc2.h, w)
                e.wait_ge(fte.h, wte)
                for tc in range(4):
                    sl = slice(tc * D, (tc + 1) * D)
                    e.tensor_scalar(f_bigA[:, sl], f_bigA[:, sl],
                                    f_cols.ap()[:, 36 + tc:37 + tc], None, OP.mult)
                    e.tensor_tensor(f_bigA[:, sl], f_bigA[:, sl],
                                    f_x1tm[:, sl], OP.add)
                for tc in range(4):
                    e.tensor_reduce(f_cols[:, 40 + tc:41 + tc],
                                    f_bigA[:, tc * D:(tc + 1) * D], AX.X, OP.add)
                e.fin(fr2.bump(), 1)
            body("vector", f)
            W_FR2 = fr2.n

            if k == 0 and "r2" in dbg:
                def f(e, w=W_FR2):
                    e.wait_ge(fr2.h, w)
                    e.dma_start(out=dbg["r2"], in_=f_bigA).then_inc(dbgs.bump(16), 16)
                body("gpsimd", f)

            def f(e, w=W_FR2):
                e.wait_ge(fr2.h, w)
                for tc in range(4):
                    e.activation(f_sqd, f_bigA[:, tc * D:(tc + 1) * D], AF.Square,
                                 accum_out=f_cols[:, 44 + tc:45 + tc])
                e.fin(fc2.bump(), 1)
            body("scalar", f)

            def f(e, w=fc2.n):
                e.wait_ge(fc2.h, w)
                e.tensor_single_scalar(f_cols[:, 48:52], f_cols[:, 40:44],
                                       1.0 / D, OP.mult)
                e.tensor_single_scalar(f_cols[:, 52:56], f_cols[:, 44:48],
                                       1.0 / D, OP.mult).then_inc(fc2.bump(), 1)
            body("vector", f)

            def f(e, w=fc2.n):
                e.wait_ge(fc2.h, w)
                e.activation(f_cols[:, 56:60], f_cols[:, 48:52],
                             AF.Square).then_inc(fc2.bump(), 1)
            body("scalar", f)

            def f(e, w=fc2.n):
                e.wait_ge(fc2.h, w)
                e.tensor_tensor(f_cols[:, 56:60], f_cols[:, 52:56],
                                f_cols[:, 56:60], OP.subtract).then_inc(fc2.bump(), 1)
            body("vector", f)

            def f(e, w=fc2.n):
                e.wait_ge(fc2.h, w)
                e.activation(f_cols[:, 60:64], f_cols[:, 56:60], AF.Ln,
                             bias=c_eps.ap()[:, 0:1])
                e.activation(f_cols[:, 60:64], f_cols[:, 60:64], AF.Exp,
                             scale=-0.5).then_inc(fc2.bump(), 1)
            body("scalar", f)

            def f(e, w=fc2.n):
                e.wait_ge(fc2.h, w)
                e.scalar_tensor_tensor(f_cols[:, 56:60], f_cols[:, 48:52], -1.0,
                                       f_cols[:, 60:64], OP.mult,
                                       OP.mult).then_inc(fc2.bump(), 1)
            body("vector", f)
            W_FC2B = fc2.n

            if k == 0 and "colsf" in dbg:
                def f(e, w=W_FC2B):
                    e.wait_ge(fc2.h, w)
                    e.dma_start(out=dbg["colsf"], in_=f_cols.ap()
                                ).then_inc(dbgs.bump(16), 16)
                body("gpsimd", f)

            for tc in range(4):
                w_fod_rot = fod.n - 16 if tc >= 2 else 0

                def f(e, tc=tc, w=W_FC2B, wfod=w_fod_rot):
                    e.wait_ge(fc2.h, w)
                    if wfod > 0:
                        e.wait_ge(fod.h, wfod)
                    e.tensor_scalar(f_out2[:, (tc % 2) * D:(tc % 2 + 1) * D],
                                    f_bigA[:, tc * D:(tc + 1) * D],
                                    f_cols.ap()[:, 60 + tc:61 + tc],
                                    f_cols.ap()[:, 56 + tc:57 + tc],
                                    OP.mult, OP.add)
                    e.fin(fo2.bump(), 1)
                body("vector", f)
                W_FO2T = fo2.n

                def f(e, k=k, tc=tc, w=W_FO2T):
                    e.wait_ge(fo2.h, w)
                    b_, half = tc // 2, tc % 2
                    e.dma_start(out=out_ext[4 * b_ + k][half * 128:(half + 1) * 128, :],
                                in_=f_out2[:, (tc % 2) * D:(tc % 2 + 1) * D]
                                ).then_inc(fod.bump(16), 16)
                body("sync", f)

        def f(e, wfod=fod.n, wag=ag.n, wq=qwd.n, wu=ud.n,
              wxs=xsd.n, wxd=xdd.n, wc=cld.n, wzd=mzd.n):
            e.wait_ge(cld.h, wc)
            e.wait_ge(ud.h, wu)
            e.wait_ge(xsd.h, wxs)
            e.wait_ge(xdd.h, wxd)
            if wq:
                e.wait_ge(qwd.h, wq)
            if wzd:
                e.wait_ge(mzd.h, wzd)
            if wag:
                e.wait_ge(ag.h, wag)
            if wfod:
                e.wait_ge(fod.h, wfod)
            if dbgs.n:
                e.wait_ge(dbgs.h, dbgs.n)
        body("sync", f)

    return nc



def _prep_inputs(inputs, core):
    x = np.asarray(inputs["L0_attn_x"])
    xb = np.zeros((NBLK, 384, D), np.float32)
    for l in range(NBLK):
        g = 8 * l + core
        b, jj = g // 32, g % 32
        t0 = jj * TBLK
        lo = max(0, t0 - 3)
        xb[l, 3 - (t0 - lo):3 + TBLK] = x[b, lo:t0 + TBLK]
    ds = slice(core * DSL, (core + 1) * DSL)
    return {
        "x_blk": xb,
        "cw": np.ascontiguousarray(
            np.asarray(inputs["L0_attn_conv_w"])[:, 0, :].reshape(16, 128, KC)),
        "cb": np.ascontiguousarray(np.asarray(inputs["L0_attn_conv_b"]).reshape(16, 128).T),
        "xw_t": np.ascontiguousarray(
            np.asarray(inputs["L0_attn_xproj_w"]).T.reshape(16, 128, 160)),
        "dtw_t": np.ascontiguousarray(np.asarray(inputs["L0_attn_dtproj_w"])[ds].T),
        "dtb": np.ascontiguousarray(
            np.asarray(inputs["L0_attn_dtproj_b"])[ds].reshape(2, 128).T),
        "alog": np.ascontiguousarray(
            np.asarray(inputs["L0_attn_A_log"])[ds].reshape(2, 128, NST)),
        "dskip": np.ascontiguousarray(
            np.asarray(inputs["L0_attn_D_skip"])[ds].reshape(2, 128).T),
        "gup": np.ascontiguousarray(
            np.asarray(inputs["L0_ffn_gate_up_packed"])[core * 1408:(core + 1) * 1408]),
        "gsc": np.ascontiguousarray(
            np.asarray(inputs["L0_ffn_gate_up_scales"])[:, core * 1408:(core + 1) * 1408]
            .T.reshape(11, 128, 16)),
        "dnp": np.ascontiguousarray(
            np.asarray(inputs["L0_ffn_down_packed"])[core * 256:(core + 1) * 256]),
        "dsc": np.ascontiguousarray(
            np.asarray(inputs["L0_ffn_down_scales"])[:, core * 256:(core + 1) * 256]
            .T.reshape(2, 128, 44)),
        "id_f32": np.eye(128, dtype=np.float32),
        "id_b16": np.eye(128, dtype=ml_dtypes.bfloat16),
        "ones_b16": np.ones((1, 128), dtype=ml_dtypes.bfloat16),
        "onescol_f32": np.ones((128, 1), dtype=np.float32),
        "onesrow_f32": np.ones((1, 128), dtype=np.float32),
        "onescol_b16": np.ones((128, 1), dtype=ml_dtypes.bfloat16),
        "eps": np.tile(np.array([[EPS_LN, EPS_RMS]], np.float32), (128, 1)),
    }


def _sel_const():
    sel = np.zeros((32, 32 * 128), dtype=ml_dtypes.bfloat16)
    for n in range(32):
        sel[n, n * 128:(n + 1) * 128] = 1.0
    return sel


_NC_CACHE = {}


_TAPS = {
    "ucc": ([D, 2048], BF16), "umine": ([2 * 8 * 128, 2048], BF16),
    "ush0": ([D, 2048], BF16),
    "wcc": ([264 * 128, 128], BF16), "xdbl": ([160, 2048], F32),
    "mu": ([128, 4 * TSTG], BF16), "mdt": ([128, B * TSTG], F32),
    "mbc": ([32, B * TSTG], F32), "delta": ([128, 4 * TSTG], BF16),
    "z0": ([B * 2 * 128, TSTG], F32), "x1": ([128, 16 * 512], F32),
    "x1tm": ([128, 4 * D], BF16), "hh": ([128, 44 * 512], BF16),
    "y2": ([128, 4 * D], BF16), "cols": ([128, 64], F32),
    "wdn": ([128, 44 * 256], BF16), "pdn": ([128, 1024], F32),
    "r2": ([128, 16 * 512], F32), "colsf": ([128, 64], F32),
    "y2e": ([128, 512], BF16), "y2g": ([128, 512], BF16),
    "evf": ([128, 512], F32), "evb": ([128, 256], BF16),
    "y2f": ([128, 16 * 512], F32),
    "ya": ([128, 512], BF16), "yb": ([128, 512], BF16), "yc": ([128, 512], BF16),
    "yd": ([128, 512], BF16), "ye": ([128, 512], BF16), "yf": ([128, 512], BF16),
    "zz": ([16 * 128, 2048], F32),
}


def kernel(**inputs):
    import os
    ph = int(os.environ.get("KPH", "6"))
    pl = int(os.environ.get("KPRE", "5"))
    taps = tuple(t for t in os.environ.get("KTAP", "").split(",") if t)
    if taps and ("y2" in taps):
        taps = taps + ("cols",)
    tap_spec = tuple((t, _TAPS[t][0], _TAPS[t][1]) for t in taps)
    key = (ph, pl, taps)
    if key not in _NC_CACHE:
        _NC_CACHE[key] = build(debug_taps=tap_spec, phases=ph, prelvl=pl)
    nc = _NC_CACHE[key]
    in_maps = [_prep_inputs(inputs, c) for c in range(NCORES)]
    res = run_bass_kernel_spmd(nc, in_maps, core_ids=list(range(NCORES)))
    if taps:
        kernel.tap_results = [{t: res.results[c]["dbg_" + t] for t in taps}
                              for c in range(NCORES)]
    out = np.zeros((B, L, D), np.float32)
    for c in range(NCORES):
        o = res.results[c]["out"]
        for l in range(NBLK):
            g = 8 * l + c
            b, jj = g // 32, g % 32
            out[b, jj * TBLK:(jj + 1) * TBLK] = o[l]
    return out
